# version 02
# baseline (speedup 1.0000x reference)
import numpy as np
import ml_dtypes
from contextlib import ExitStack
import concourse.bass as bass
import concourse.mybir as mybir
from concourse.bass_utils import run_bass_kernel_spmd

F32 = mybir.dt.float32
BF16 = mybir.dt.bfloat16
AF = mybir.ActivationFunctionType
ALU = mybir.AluOpType
AX = mybir.AxisListType

L = 2048
D = 2048
NF = 16
H = 32
HD = 64
INTER = 8192
NI = 64
NO = 16
NT = 16
NQC = 4
NHALF = 2
TW = 1024
C2048 = float(1.0 / (64.0 * np.sqrt(2048.0)))
C8192 = float(1.0 / (64.0 * np.sqrt(8192.0)))
BF = ml_dtypes.bfloat16


def _sem_reset_range(nc):
    ksr = nc._kernel_sem_range
    start = ksr.start
    mono = start + (3 if nc._bir_kernel_barrier_sem is not None else 2)
    return range(mono + len(nc._monotonic_sems), ksr.stop)


def _hadamard(n):
    h = np.array([[1.0]], dtype=np.float32)
    while h.shape[0] < n:
        h = np.block([[h, h], [h, -h]])
    return h


def build_host_consts():
    ident = np.eye(128, dtype=BF)
    h128 = _hadamard(128).astype(BF)
    p2 = np.zeros((128, 128), dtype=np.float32)
    for d in range(128):
        e = (d % 64 + 32) % 64 + 64 * (d // 64)
        p2[e, d] = 1.0
    p2 = p2.astype(BF)
    ki, qi = np.meshgrid(np.arange(128), np.arange(128), indexing="ij")
    tri = (ki <= qi).astype(np.float32).astype(BF)
    inv_freq = 1.0 / 512.0 ** (np.arange(0, HD, 2, dtype=np.float32) / HD)
    t = np.arange(L, dtype=np.float32)
    didx = np.arange(128) % 64
    fr = t[None, :] * inv_freq[didx % 32][:, None]
    cosT = np.cos(fr).astype(BF)
    sgn = np.where(didx < 32, -1.0, 1.0).astype(np.float32)
    sinT = (np.sin(fr) * sgn[:, None]).astype(BF)
    return dict(ident=ident, h128=h128, p2=p2, tri=tri,
                cosT=np.ascontiguousarray(cosT), sinT=np.ascontiguousarray(sinT))


SER = 0

def build_program(debug=False):
    nc = bass.Bass("TRN2", target_bir_lowering=False)

    xb = nc.dram_tensor("xb", [L, D], F32, kind="ExternalInput").ap()
    wqkv_blk = nc.dram_tensor("wqkv_blk", [128, 32, NF, 128], F32, kind="ExternalInput").ap()
    wv_blk = nc.dram_tensor("wv_blk", [128, 4, NF, 512], F32, kind="ExternalInput").ap()
    wo_blk = nc.dram_tensor("wo_blk", [128, NF, D], F32, kind="ExternalInput").ap()
    wu_blk = nc.dram_tensor("wu_blk", [128, NI, NF, 128], F32, kind="ExternalInput").ap()
    wg_blk = nc.dram_tensor("wg_blk", [128, NI, NF, 128], F32, kind="ExternalInput").ap()
    wd_blk = nc.dram_tensor("wd_blk", [128, NO, NI, 128], F32, kind="ExternalInput").ap()
    su_ug_pt = nc.dram_tensor("su_ug_pt", [128, NF], F32, kind="ExternalInput").ap()
    su_dp_pt = nc.dram_tensor("su_dp_pt", [128, NI], F32, kind="ExternalInput").ap()
    wsu_pt = nc.dram_tensor("wsu_pt", [128, NI], F32, kind="ExternalInput").ap()
    wsg_pt = nc.dram_tensor("wsg_pt", [128, NI], F32, kind="ExternalInput").ap()
    wsdp_pt = nc.dram_tensor("wsdp_pt", [128, NO], F32, kind="ExternalInput").ap()
    ident_in = nc.dram_tensor("ident_in", [128, 128], BF16, kind="ExternalInput").ap()
    h128_in = nc.dram_tensor("h128_in", [128, 128], BF16, kind="ExternalInput").ap()
    p2_in = nc.dram_tensor("p2_in", [128, 128], BF16, kind="ExternalInput").ap()
    tri_in = nc.dram_tensor("tri_in", [128, 128], BF16, kind="ExternalInput").ap()
    cos_in = nc.dram_tensor("cos_in", [128, L], BF16, kind="ExternalInput").ap()
    sin_in = nc.dram_tensor("sin_in", [128, L], BF16, kind="ExternalInput").ap()
    out = nc.dram_tensor("out", [L, D], F32, kind="ExternalOutput").ap()

    xn_scr = nc.dram_tensor("xn_scr", [NF, 128, L], BF16, kind="Internal").ap()
    q_scr = nc.dram_tensor("q_scr", [NF, 128, L], BF16, kind="Internal").ap()
    v_scr = nc.dram_tensor("v_scr", [NT, 128, 32 * 65], BF16, kind="Internal").ap()
    at_scr = nc.dram_tensor("at_scr", [NF, 128, L], BF16, kind="Internal").ap()
    s_scr = nc.dram_tensor("s_scr", [NT, 128, D], BF16, kind="Internal").ap()
    g_scr = nc.dram_tensor("g_scr", [NO, 128, TW], BF16, kind="Internal").ap()

    dbg = {}
    if debug:
        for nm, shp in [("xnT0", [128, L]), ("kT0", [128, L]), ("q0", [128, L]),
                        ("v0", [128, 32 * 65]), ("att0", [128, L]), ("s0", [128, D]),
                        ("y0", [128, TW]), ("h0", [128, TW]), ("g0", [128, TW]),
                        ("att_all", [NF, 128, L]), ("s_all", [NT, 128, D]),
                        ("q_all", [NF, 128, L]), ("v_all", [NT, 128, 32 * 65]),
                        ("k_all", [NF, 128, L]), ("atok0", [128, D])]:
            dbg[nm] = nc.dram_tensor("dbg_" + nm, shp, BF16, kind="ExternalOutput").ap()

    te, ve, se, ge, sy = nc.tensor, nc.vector, nc.scalar, nc.gpsimd, nc.sync

    ctx = ExitStack()
    SB = lambda n, shp, dt: ctx.enter_context(nc.sbuf_tensor(n, shp, dt))
    sem = lambda n: ctx.enter_context(nc.semaphore(n))

    sW, sX, sT, sS, sV, sR = (sem(x) for x in ("sW", "sX", "sT", "sS", "sV", "sR"))
    cnt = {"W": 0, "X": 0, "T": 0, "S": 0, "V": 0, "R": 0}
    SEMS = {"W": sW, "X": sX, "T": sT, "S": sS, "V": sV, "R": sR}

    def inc(key, instr, n=None):
        n = (16 if key in ("W", "X") else 1) if n is None else n
        cnt[key] += n
        instr.then_inc(SEMS[key], n)
        return cnt[key]

    def wait(eng, key, val=None):
        v = int(cnt[key] if val is None else val)
        if v > 0:
            eng.wait_ge(SEMS[key], v)

    ident = SB("ident", [128, 128], BF16)
    h128c = SB("h128c", [128, 128], BF16)
    p2c = SB("p2c", [128, 128], BF16)
    tric = SB("tric", [128, 128], BF16)
    su_ug = SB("su_ug", [128, NF], F32)
    wsu64su = SB("wsu64su", [128, NI], F32)
    wsg64 = SB("wsg64", [128, NI], F32)
    wsdp64 = SB("wsdp64", [128, NO], F32)
    h128su = SB("h128su", [128, NF * 128], BF16)
    tmp_pc = SB("tmp_pc", [128, NI], F32)
    su_dp_sb = SB("su_dp_sb", [128, NI], F32)
    wsu_sb = SB("wsu_sb", [128, NI], F32)

    rr = _sem_reset_range(nc)
    ge.dma_reset(rr)
    ge.sem_clear(rr)
    nc.all_engine_barrier()

    for src, dst in [(ident_in, ident), (h128_in, h128c), (p2_in, p2c), (tri_in, tric)]:
        inc("X", sy.dma_start(out=dst[:, :], in_=src[:, :]))
    inc("X", sy.dma_start(out=su_ug[:, :], in_=su_ug_pt[:, :]))
    inc("X", sy.dma_start(out=su_dp_sb[:, :], in_=su_dp_pt[:, :]))
    inc("X", sy.dma_start(out=wsu_sb[:, :], in_=wsu_pt[:, :]))
    inc("X", sy.dma_start(out=wsg64[:, :], in_=wsg_pt[:, :]))
    inc("X", sy.dma_start(out=wsdp64[:, :], in_=wsdp_pt[:, :]))
    wait(ve, "X")
    e0_ = inc("R", ve.tensor_scalar(tmp_pc[:, :], wsu_sb[:, :], 64.0, None, ALU.mult))
    ve.wait_ge(sR, e0_)
    ve.tensor_tensor(wsu64su[:, :], tmp_pc[:, :], su_dp_sb[:, :], ALU.mult)
    ve.tensor_scalar(wsg64[:, :], wsg64[:, :], 64.0, None, ALU.mult)
    ve.tensor_scalar(wsdp64[:, :], wsdp64[:, :], 64.0, None, ALU.mult)
    for fc in range(NF):
        ve.tensor_scalar(h128su[:, fc * 128:(fc + 1) * 128], h128c[:, :],
                         su_ug[:, fc:fc + 1], None, ALU.mult)
    nc.all_engine_barrier()

    with ExitStack() as scB:
        kT = [scB.enter_context(nc.sbuf_tensor(f"kT{kd}", [128, L], BF16))
              for kd in range(NF)]

        with ExitStack() as scB1:
            xnT = [scB1.enter_context(nc.sbuf_tensor(f"xnT{fc}", [128, L], BF16))
                   for fc in range(NF)]

            with ExitStack() as sc1:
                SB1 = lambda n, shp, dt: sc1.enter_context(nc.sbuf_tensor(n, shp, dt))
                PS1 = lambda n, shp, dt: sc1.enter_context(nc.psum_tensor(n, shp, dt))
                xbuf = [SB1(f"xbuf{i}", [128, D], F32) for i in range(2)]
                sq_scr = SB1("sq_scr", [128, D], F32)
                s_sum = SB1("s_sum", [128, 4], F32)
                s_mu = SB1("s_mu", [128, 4], F32)
                s_mu2 = SB1("s_mu2", [128, 4], F32)
                s_bias = SB1("s_bias", [128, 4], F32)
                s_sq = SB1("s_sq", [128, 4], F32)
                s_rstd = SB1("s_rstd", [128, 4], F32)
                s_std = SB1("s_std", [128, 4], F32)
                xn_tok = SB1("xn_tok", [128, D], BF16)
                ps_tr = [PS1(f"ps_tr{i}", [128, 128], BF16) for i in range(2)]

                apply_done = [0, 0]
                evac_done = [0, 0]
                for tt in range(NT):
                    if tt >= 2:
                        wait(sy, "V", apply_done[tt % 2])
                    x_done = inc("X", sy.dma_start(out=xbuf[tt % 2][:, :],
                                                   in_=xb[tt * 128:(tt + 1) * 128, :]))
                    wait(ve, "X", x_done)
                    e1 = inc("R", ve.tensor_reduce(s_sum[:, 0:1], xbuf[tt % 2][:, :],
                                                   AX.X, ALU.add))
                    ve.wait_ge(sR, e1)
                    e2 = inc("R", ve.tensor_scalar(s_mu[:, 0:1], s_sum[:, 0:1],
                                                   1.0 / D, None, ALU.mult))
                    ve.wait_ge(sR, e2)
                    e3 = inc("R", ve.tensor_tensor(s_mu2[:, 0:1], s_mu[:, 0:1],
                                                   s_mu[:, 0:1], ALU.mult))
                    ve.wait_ge(sR, e3)
                    vb = inc("V", ve.tensor_scalar(s_bias[:, 0:1], s_mu2[:, 0:1],
                                                   -1.0, 1e-5, ALU.mult, ALU.add))
                    wait(se, "X", x_done)
                    se.activation(sq_scr[:, :], xbuf[tt % 2][:, :], AF.Square,
                                  accum_out=s_sq[:, 0:1])
                    wait(se, "V", vb)
                    sr = inc("S", se.activation(s_std[:, 0:1], s_sq[:, 0:1], AF.Sqrt,
                                                bias=s_bias[:, 0:1], scale=1.0 / D))
                    wait(ve, "S", sr)
                    rr_ = inc("R", ve.reciprocal(s_rstd[:, 0:1], s_std[:, 0:1]))
                    ve.wait_ge(sR, rr_)
                    va = inc("V", ve.tensor_scalar(xn_tok[:, :], xbuf[tt % 2][:, :],
                                                   s_mu[:, 0:1], s_rstd[:, 0:1],
                                                   ALU.subtract, ALU.mult))
                    apply_done[tt % 2] = va
                    wait(te, "V", va)
                    for j in range(NF):
                        wait(te, "V", evac_done[j % 2])
                        tj = inc("T", te.transpose(ps_tr[j % 2][:, :],
                                                   xn_tok[:, j * 128:(j + 1) * 128],
                                                   ident[:, :]))
                        wait(ve, "T", tj)
                        evac_done[j % 2] = inc("V", ve.tensor_copy(
                            xnT[j][:, tt * 128:(tt + 1) * 128], ps_tr[j % 2][:, :]))
                for fc in range(NF):
                    wait(sy, "V")
                    inc("X", sy.dma_start(out=xn_scr[fc], in_=xnT[fc][:, :]))
                if debug:
                    inc("X", sy.dma_start(out=dbg["xnT0"][:, :], in_=xnT[0][:, :]))
                wait(sy, "X")
            nc.all_engine_barrier()

            with ExitStack() as sc2:
                SB2 = lambda n, shp, dt: sc2.enter_context(nc.sbuf_tensor(n, shp, dt))
                PS2 = lambda n, shp, dt: sc2.enter_context(nc.psum_tensor(n, shp, dt))
                ones_t = SB2("ones_t", [128, 32 * 65], BF16)
                wvb = [SB2(f"wvb{i}", [128, NF * 256], BF16) for i in range(2)]
                vstage = [SB2(f"vstage{i}", [128, 256], BF16) for i in range(2)]
                ps_v = [PS2(f"ps_v{i}", [128, 256], F32) for i in range(2)]
                on_ = inc("V", ve.memset(ones_t[:, :], 1.0))
                wait(sy, "V", on_)
                for tt in range(NT):
                    inc("X", sy.dma_start(out=v_scr[tt], in_=ones_t[:, :]))
                NVC = 8
                wv_used = [0, 0]
                vst_dma = [0, 0]
                vst_evac = [0, 0]
                for ch in range(NVC):
                    vc, half = ch // 2, ch % 2
                    if ch >= 2:
                        wait(ge, "T", wv_used[ch % 2])
                    wvd = inc("W", ge.dma_start(
                        out=wvb[ch % 2][:, :].rearrange("p (f j) -> p f j", f=NF),
                        in_=wv_blk[:, vc, :, half * 256:(half + 1) * 256]))
                    for tt in range(NT):
                        wait(te, "W", wvd)
                        i2 = tt % 2
                        wait(te, "V", vst_evac[i2])
                        grp = None
                        for fc in range(NF):
                            grp = te.matmul(ps_v[i2][:, :],
                                            xnT[fc][:, tt * 128:(tt + 1) * 128],
                                            wvb[ch % 2][:, fc * 256:(fc + 1) * 256],
                                            start=(fc == 0), stop=(fc == NF - 1))
                        tg = inc("T", grp)
                        wv_used[ch % 2] = tg
                        wait(ve, "T", tg)
                        wait(ve, "X", vst_dma[i2])
                        ev = inc("V", ve.tensor_copy(vstage[i2][:, :], ps_v[i2][:, :]))
                        vst_evac[i2] = ev
                        wait(sy, "V", ev)
                        vst_dma[i2] = inc("X", sy.dma_start(
                            out=v_scr[tt].rearrange("p (h c) -> p h c", c=65)[
                                :, ch * 4:(ch + 1) * 4, 0:64],
                            in_=vstage[i2][:, :].rearrange("p (h c) -> p h c", c=64)))
                wait(sy, "X")
            nc.all_engine_barrier()

            with ExitStack() as scQ:
                SBQ = lambda n, shp, dt: scQ.enter_context(nc.sbuf_tensor(n, shp, dt))
                PSQ = lambda n, shp, dt: scQ.enter_context(nc.psum_tensor(n, shp, dt))
                cosT = SBQ("cosT", [128, L], BF16)
                sinT = SBQ("sinT", [128, L], BF16)
                wkb = [SBQ(f"wkb{i}", [128, D], BF16) for i in range(2)]
                raw = [SBQ(f"raw{i}", [128, 512], BF16) for i in range(2)]
                rot = [SBQ(f"rot{i}", [128, 512], BF16) for i in range(2)]
                tmpa = SBQ("tmpa", [128, 512], BF16)
                tmpb = SBQ("tmpb", [128, 512], BF16)
                qstage = [SBQ(f"qstage{i}", [128, L], BF16) for i in range(2)]
                ps_k = [PSQ(f"ps_k{i}", [128, 512], F32) for i in range(2)]
                ps_r = [PSQ(f"ps_r{i}", [128, 512], F32) for i in range(2)]
                inc("X", sy.dma_start(out=cosT[:, :], in_=cos_in[:, :]))
                inc("X", sy.dma_start(out=sinT[:, :], in_=sin_in[:, :]))
                rope_ready = cnt["X"]

                def qk_phase(is_q):
                    entry_T = cnt["T"]
                    wk_used = [entry_T, entry_T]
                    q_spill = [0, 0]
                    rope_done = [0, 0]
                    for dt_i in range(NF):
                        blk_idx = dt_i if is_q else (16 + dt_i)
                        wait(ge, "T", wk_used[dt_i % 2])
                        wd = inc("W", ge.dma_start(
                            out=wkb[dt_i % 2][:, :].rearrange("p (f j) -> p f j", f=NF),
                            in_=wqkv_blk[:, blk_idx]))
                        dstT = qstage[dt_i % 2] if is_q else kT[dt_i]
                        vv = 0
                        for tc in range(4):
                            wait(te, "W", wd)
                            wait(te, "V", rope_done[tc % 2])
                            grp = None
                            for fc in range(NF):
                                grp = te.matmul(ps_k[tc % 2][:, :],
                                                wkb[dt_i % 2][:, fc * 128:(fc + 1) * 128],
                                                xnT[fc][:, tc * 512:(tc + 1) * 512],
                                                start=(fc == 0), stop=(fc == NF - 1))
                            tg = inc("T", grp)
                            wk_used[dt_i % 2] = tg
                            wait(se, "T", tg)
                            if is_q:
                                wait(se, "X", q_spill[dt_i % 2])
                            sc = inc("S", se.activation(raw[tc % 2][:, :],
                                                        ps_k[tc % 2][:, :], AF.Copy))
                            wait(te, "S", sc)
                            tr = inc("T", te.matmul(ps_r[tc % 2][:, :], p2c[:, :],
                                                    raw[tc % 2][:, :],
                                                    start=True, stop=True))
                            wait(ve, "S", sc)
                            wait(ve, "X", rope_ready)
                            ve.tensor_tensor(tmpa[:, :], raw[tc % 2][:, :],
                                             cosT[:, tc * 512:(tc + 1) * 512], ALU.mult)
                            wait(ve, "T", tr)
                            ve.tensor_copy(rot[tc % 2][:, :], ps_r[tc % 2][:, :])
                            ve.tensor_tensor(tmpb[:, :], rot[tc % 2][:, :],
                                             sinT[:, tc * 512:(tc + 1) * 512], ALU.mult)
                            vv = inc("V", ve.tensor_tensor(
                                dstT[:, tc * 512:(tc + 1) * 512],
                                tmpa[:, :], tmpb[:, :], ALU.add))
                            rope_done[tc % 2] = vv
                        if is_q:
                            wait(sy, "V", vv)
                            q_spill[dt_i % 2] = inc("X", sy.dma_start(
                                out=q_scr[dt_i], in_=dstT[:, :]))
                    if is_q:
                        wait(sy, "X")

                qk_phase(False)
                qk_phase(True)
                if debug:
                    wait(sy, "V")
                    inc("X", sy.dma_start(out=dbg["kT0"][:, :], in_=kT[0][:, :]))
                    for i_ in range(NF):
                        inc("X", sy.dma_start(out=dbg["k_all"][i_], in_=kT[i_][:, :]))
            nc.all_engine_barrier()

        with ExitStack() as sc3:
            SB3 = lambda n, shp, dt: sc3.enter_context(nc.sbuf_tensor(n, shp, dt))
            PS3 = lambda n, shp, dt: sc3.enter_context(nc.psum_tensor(n, shp, dt))
            v_aug = [SB3(f"vaug{tt}", [128, 32 * 65], BF16) for tt in range(NT)]
            qbuf = [SB3(f"qbuf{qd}", [128, 512], BF16) for qd in range(NF)]
            probs = [[SB3(f"probs{pp}_{kt}", [128, 512], BF16) for kt in range(NT)]
                     for pp in range(2)]
            attn_tok = [SB3(f"attn_tok{qs}", [128, D], BF16) for qs in range(4)]
            atb = [SB3(f"atb{i}", [128, 128], BF16) for i in range(2)]
            rec = SB3("rec", [128, 16], F32)
            ps_s = [PS3(f"ps_s{i}", [128, 512], F32) for i in range(2)]
            ps_av = [PS3(f"ps_av{i}", [128, 65], F32) for i in range(2)]
            ps_tr3 = [PS3(f"ps_tr3{i}", [128, 128], BF16) for i in range(2)]

            for tt in range(NT):
                inc("X", sy.dma_start(out=v_aug[tt][:, :], in_=v_scr[tt]))
            v_ready = cnt["X"]

            exp_done = np.zeros((2, NT), dtype=np.int64)
            exp_bank = [0, 0]
            mask_done = np.zeros((2, NT), dtype=np.int64)
            av_last_user = np.zeros((2, NT), dtype=np.int64)
            norm_done = [0, 0]
            atb_dma = [0, 0]
            first = True
            for qc in range(NQC):
                if qc >= 1:
                    wait(sy, "T")
                qld = [0] * NF
                for qd in range(NF):
                    qld[qd] = inc("X", sy.dma_start(
                        out=qbuf[qd][:, :], in_=q_scr[qd][:, qc * 512:(qc + 1) * 512]))
                nkt = 4 * qc + 4
                hseq = ([0] if qc == 0 else []) + list(range(H))
                for h_i in hseq:
                    pp = h_i % 2
                    kd, off = h_i // 2, (h_i % 2) * 64
                    for kt in range(nkt):
                        wait(te, "X", qld[kd])
                        if first:
                            wait(te, "X", v_ready)
                            first = False
                        wait(te, "S", exp_bank[kt % 2])
                        tg = inc("T", te.matmul(
                            ps_s[kt % 2][:, :],
                            kT[kd][off:off + 64, kt * 128:(kt + 1) * 128],
                            qbuf[kd][off:off + 64, :], start=True, stop=True))
                        wait(se, "T", tg)
                        wait(se, "T", av_last_user[pp][kt])
                        ex = inc("S", se.activation(probs[pp][kt][:, :],
                                                    ps_s[kt % 2][:, :],
                                                    AF.Exp, scale=0.125))
                        exp_done[pp][kt] = ex
                        exp_bank[kt % 2] = ex
                        if kt >= 4 * qc:
                            j = kt - 4 * qc
                            wait(ve, "S", ex)
                            mask_done[pp][kt] = inc("V", ve.tensor_tensor(
                                probs[pp][kt][:, j * 128:(j + 1) * 128],
                                probs[pp][kt][:, j * 128:(j + 1) * 128],
                                tric[:, :], ALU.mult))
                    for qs in range(4):
                        qs128 = qc * 4 + qs
                        if SER:
                            wait(te, "S")
                            wait(te, "V")
                        wait(te, "S", exp_done[pp][qs128])
                        wait(te, "V", mask_done[pp][qs128])
                        wait(te, "V", norm_done[qs % 2])
                        grp = None
                        for kt in range(qs128 + 1):
                            grp = te.matmul(ps_av[qs % 2][:, :],
                                            probs[pp][kt][:, qs * 128:(qs + 1) * 128],
                                            v_aug[kt][:, h_i * 65:h_i * 65 + 65],
                                            start=(kt == 0), stop=(kt == qs128))
                        avg = inc("T", grp)
                        for kt in range(qs128 + 1):
                            av_last_user[pp][kt] = avg
                        wait(ve, "T", avg)
                        rr_ = inc("R", ve.reciprocal(rec[:, qs * 4:qs * 4 + 1],
                                                     ps_av[qs % 2][:, 64:65]))
                        ve.wait_ge(sR, rr_)
                        norm_done[qs % 2] = inc("V", ve.tensor_scalar(
                            attn_tok[qs][:, h_i * 64:(h_i + 1) * 64],
                            ps_av[qs % 2][:, 0:64],
                            rec[:, qs * 4:qs * 4 + 1], None, ALU.mult))
                if debug and qc == 0:
                    wait(sy, "V")
                    inc("X", sy.dma_start(out=dbg["atok0"][:, :], in_=attn_tok[0][:, :]))
                    wait(sy, "X")
                wait(te, "V")
                tr_evac = [0, 0]
                for qs in range(4):
                    for dt_i in range(NF):
                        i2 = dt_i % 2
                        wait(te, "V", tr_evac[i2])
                        tg = inc("T", te.transpose(
                            ps_tr3[i2][:, :],
                            attn_tok[qs][:, dt_i * 128:(dt_i + 1) * 128], ident[:, :]))
                        wait(ve, "T", tg)
                        wait(ve, "X", atb_dma[i2])
                        tr_evac[i2] = inc("V", ve.tensor_copy(atb[i2][:, :],
                                                              ps_tr3[i2][:, :]))
                        wait(sy, "V", tr_evac[i2])
                        atb_dma[i2] = inc("X", sy.dma_start(
                            out=at_scr[dt_i][:, qc * 512 + qs * 128:
                                             qc * 512 + (qs + 1) * 128],
                            in_=atb[i2][:, :]))
            wait(sy, "X")
            if debug:
                inc("X", sy.dma_start(out=dbg["att0"][:, :], in_=at_scr[0]))
                inc("X", sy.dma_start(out=dbg["q0"][:, :], in_=q_scr[0]))
                inc("X", sy.dma_start(out=dbg["v0"][:, :], in_=v_scr[0]))
                for i_ in range(NF):
                    inc("X", sy.dma_start(out=dbg["att_all"][i_], in_=at_scr[i_]))
                    inc("X", sy.dma_start(out=dbg["q_all"][i_], in_=q_scr[i_]))
                for i_ in range(NT):
                    inc("X", sy.dma_start(out=dbg["v_all"][i_], in_=v_scr[i_]))
        nc.all_engine_barrier()

    with ExitStack() as sc4:
        SB4 = lambda n, shp, dt: sc4.enter_context(nc.sbuf_tensor(n, shp, dt))
        PS4 = lambda n, shp, dt: sc4.enter_context(nc.psum_tensor(n, shp, dt))
        woT = SB4("woT", [128, NF * 2048], BF16)
        attb = [SB4(f"attb{dt_i}", [128, 512], BF16) for dt_i in range(NF)]
        s_stage = [SB4(f"s_stage{i}", [128, D], BF16) for i in range(2)]
        ps_wo = [PS4(f"ps_wo{i}", [128, 512], F32) for i in range(2)]
        wo_done = 0
        for g4 in range(4):
            wo_done = inc("W", ge.dma_start(
                out=woT[:, g4 * 4 * 2048:(g4 + 1) * 4 * 2048].rearrange(
                    "p (f j) -> p f j", f=4),
                in_=wo_blk[:, g4 * 4:(g4 + 1) * 4]))
        sp_done = [0, 0]
        ev_bank = [0, 0]
        for qc in range(NQC):
            if qc >= 1:
                wait(sy, "T")
            for dt_i in range(NF):
                inc("X", sy.dma_start(out=attb[dt_i][:, :],
                                      in_=at_scr[dt_i][:, qc * 512:(qc + 1) * 512]))
            att_ready = cnt["X"]
            for qs in range(4):
                ev = 0
                for oc in range(4):
                    wait(te, "W", wo_done)
                    wait(te, "X", att_ready)
                    wait(te, "S", ev_bank[oc % 2])
                    grp = None
                    for fc in range(NF):
                        grp = te.matmul(
                            ps_wo[oc % 2][:, :],
                            attb[fc][:, qs * 128:(qs + 1) * 128],
                            woT[:, fc * 2048 + oc * 512:fc * 2048 + (oc + 1) * 512],
                            start=(fc == 0), stop=(fc == NF - 1))
                    tg = inc("T", grp)
                    wait(se, "T", tg)
                    if oc == 0:
                        wait(se, "X", sp_done[qs % 2])
                    ev = inc("S", se.activation(
                        s_stage[qs % 2][:, oc * 512:(oc + 1) * 512],
                        ps_wo[oc % 2][:, :], AF.Copy))
                    ev_bank[oc % 2] = ev
                wait(sy, "S", ev)
                sp_done[qs % 2] = inc("X", sy.dma_start(out=s_scr[qc * 4 + qs],
                                                        in_=s_stage[qs % 2][:, :]))
        wait(sy, "X")
        if debug:
            inc("X", sy.dma_start(out=dbg["s0"][:, :], in_=s_scr[0]))
            for i_ in range(NT):
                inc("X", sy.dma_start(out=dbg["s_all"][i_], in_=s_scr[i_]))
    nc.all_engine_barrier()

    for th in range(NHALF):
        tb = th * TW
        with ExitStack() as sc5h:
            SBH = lambda n, shp, dt: sc5h.enter_context(nc.sbuf_tensor(n, shp, dt))
            hbuf = [SBH(f"h{th}_{it}", [128, TW], BF16) for it in range(NI)]

            with ExitStack() as sc6:
                SB6 = lambda n, shp, dt: sc6.enter_context(nc.sbuf_tensor(n, shp, dt))
                PS6 = lambda n, shp, dt: sc6.enter_context(nc.psum_tensor(n, shp, dt))
                ybuf = [SB6(f"y{th}_{fc}", [128, TW], BF16) for fc in range(NF)]
                xnb = [SB6(f"xnb{th}_{i}", [128, TW], BF16) for i in range(2)]
                wub = [SB6(f"wub{th}_{i}", [128, D], BF16) for i in range(2)]
                wgb = [SB6(f"wgb{th}_{i}", [128, D], BF16) for i in range(2)]
                sigb = [SB6(f"sigb{th}_{i}", [128, 512], F32) for i in range(2)]
                t1b = [SB6(f"t1b{th}_{i}", [128, 512], F32) for i in range(2)]
                ps_u = [PS6(f"ps_u{th}_{i}", [128, 512], F32) for i in range(2)]
                ps_g = [PS6(f"ps_g{th}_{i}", [128, 512], F32) for i in range(2)]

                xn_used = [0, 0]
                yev_bank = [0, 0]
                yev = 0
                for fc in range(NF):
                    if fc >= 2:
                        wait(sy, "T", xn_used[fc % 2])
                    xld = inc("X", sy.dma_start(out=xnb[fc % 2][:, :],
                                                in_=xn_scr[fc][:, tb:tb + TW]))
                    for t2 in range(2):
                        wait(te, "X", xld)
                        wait(te, "S", yev_bank[t2])
                        tg = inc("T", te.matmul(ps_u[t2][:, :],
                                                h128su[:, fc * 128:(fc + 1) * 128],
                                                xnb[fc % 2][:, t2 * 512:(t2 + 1) * 512],
                                                start=True, stop=True))
                        xn_used[fc % 2] = tg
                        wait(se, "T", tg)
                        yev = inc("S", se.activation(
                            ybuf[fc][:, t2 * 512:(t2 + 1) * 512],
                            ps_u[t2][:, :], AF.Copy, scale=C2048))
                        yev_bank[t2] = yev
                wait(ve, "S", yev)
                hh = 1
                while hh < NF:
                    for base in range(0, NF, 2 * hh):
                        for j in range(hh):
                            a, b = ybuf[base + j], ybuf[base + hh + j]
                            ve.tensor_tensor(a[:, :], a[:, :], b[:, :], ALU.add)
                            ve.scalar_tensor_tensor(b[:, :], b[:, :], -2.0, a[:, :],
                                                    ALU.mult, ALU.add)
                    hh *= 2
                y_ready = inc("V", ve.tensor_copy(tmp_pc[:, 0:1], tmp_pc[:, 0:1]))
                if debug and th == 0:
                    wait(sy, "V", y_ready)
                    inc("X", sy.dma_start(out=dbg["y0"][:, :], in_=ybuf[0][:, :]))

                wug_used = [0, 0]
                hdone = 0
                for it in range(NI):
                    if it >= 2:
                        wait(ge, "T", wug_used[it % 2])
                    wu = inc("W", ge.dma_start(out=wub[it % 2][:, :].rearrange(
                        "p (f j) -> p f j", f=NF), in_=wu_blk[:, it]))
                    wg = inc("W", ge.dma_start(out=wgb[it % 2][:, :].rearrange(
                        "p (f j) -> p f j", f=NF), in_=wg_blk[:, it]))
                    wait(te, "V", y_ready)
                    for t2 in range(2):
                        wait(te, "W", wg)
                        wait(te, "V", hdone)
                        for fc in range(NF):
                            te.matmul(ps_u[t2][:, :],
                                      wub[it % 2][:, fc * 128:(fc + 1) * 128],
                                      ybuf[fc][:, t2 * 512:(t2 + 1) * 512],
                                      start=(fc == 0), stop=(fc == NF - 1))
                        gg = None
                        for fc in range(NF):
                            gg = te.matmul(ps_g[t2][:, :],
                                           wgb[it % 2][:, fc * 128:(fc + 1) * 128],
                                           ybuf[fc][:, t2 * 512:(t2 + 1) * 512],
                                           start=(fc == 0), stop=(fc == NF - 1))
                        tg = inc("T", gg)
                        wug_used[it % 2] = tg
                        wait(se, "T", tg)
                        sg_ = inc("S", se.activation(sigb[t2][:, :], ps_g[t2][:, :],
                                                     AF.Sigmoid,
                                                     scale=wsg64[:, it:it + 1]))
                        wait(ve, "S", sg_)
                        ve.scalar_tensor_tensor(t1b[t2][:, :], ps_g[t2][:, :],
                                                wsg64[:, it:it + 1], sigb[t2][:, :],
                                                ALU.mult, ALU.mult)
                        hdone = inc("V", ve.scalar_tensor_tensor(
                            hbuf[it][:, t2 * 512:(t2 + 1) * 512], ps_u[t2][:, :],
                            wsu64su[:, it:it + 1], t1b[t2][:, :], ALU.mult, ALU.mult))

                hev_bank = [0, 0]
                hev = 0
                wait(te, "V", hdone)
                for it in range(NI):
                    for t2 in range(2):
                        wait(te, "S", hev_bank[t2])
                        tg = inc("T", te.matmul(ps_u[t2][:, :], h128c[:, :],
                                                hbuf[it][:, t2 * 512:(t2 + 1) * 512],
                                                start=True, stop=True))
                        wait(se, "T", tg)
                        hev = inc("S", se.activation(
                            hbuf[it][:, t2 * 512:(t2 + 1) * 512],
                            ps_u[t2][:, :], AF.Copy, scale=C8192))
                        hev_bank[t2] = hev
                wait(ve, "S", hev)
                hh = 1
                while hh < NI:
                    for base in range(0, NI, 2 * hh):
                        for j in range(hh):
                            a, b = hbuf[base + j], hbuf[base + hh + j]
                            ve.tensor_tensor(a[:, :], a[:, :], b[:, :], ALU.add)
                            ve.scalar_tensor_tensor(b[:, :], b[:, :], -2.0, a[:, :],
                                                    ALU.mult, ALU.add)
                    hh *= 2
                h_ready = inc("V", ve.tensor_copy(tmp_pc[:, 0:1], tmp_pc[:, 0:1]))
                if debug and th == 0:
                    wait(sy, "V", h_ready)
                    inc("X", sy.dma_start(out=dbg["h0"][:, :], in_=hbuf[0][:, :]))

            with ExitStack() as sc7:
                SB7 = lambda n, shp, dt: sc7.enter_context(nc.sbuf_tensor(n, shp, dt))
                PS7 = lambda n, shp, dt: sc7.enter_context(nc.psum_tensor(n, shp, dt))
                wdb = [SB7(f"wdb{th}_{i}", [128, 32 * 128], BF16) for i in range(2)]
                gstage = [SB7(f"gstage{th}_{i}", [128, TW], BF16) for i in range(2)]
                sigd = [SB7(f"sigd{th}_{i}", [128, 512], F32) for i in range(2)]
                ps_d = [PS7(f"ps_d{th}_{i}", [128, 512], F32) for i in range(2)]
                wd_used = [0, 0]
                g_spill = [0, 0]
                wait(ge, "T")
                for ot in range(NO):
                    wd_done = [0, 0]
                    for hf in range(2):
                        if ot >= 1:
                            wait(ge, "T", wd_used[hf])
                        wd_done[hf] = inc("W", ge.dma_start(
                            out=wdb[hf][:, :].rearrange("p (i j) -> p i j", i=32),
                            in_=wd_blk[:, ot, hf * 32:(hf + 1) * 32]))
                    wait(te, "V", h_ready)
                    gdone = 0
                    for t2 in range(2):
                        wait(te, "V", gdone)
                        grp = None
                        for it in range(NI):
                            hf, il = it // 32, it % 32
                            if il == 0:
                                wait(te, "W", wd_done[hf])
                            grp = te.matmul(ps_d[t2][:, :],
                                            wdb[hf][:, il * 128:(il + 1) * 128],
                                            hbuf[it][:, t2 * 512:(t2 + 1) * 512],
                                            start=(it == 0), stop=(it == NI - 1))
                        tg = inc("T", grp)
                        wd_used[0] = wd_used[1] = tg
                        wait(se, "T", tg)
                        sd = inc("S", se.activation(sigd[t2][:, :], ps_d[t2][:, :],
                                                    AF.Sigmoid,
                                                    scale=wsdp64[:, ot:ot + 1]))
                        wait(ve, "S", sd)
                        if t2 == 0:
                            wait(ve, "X", g_spill[ot % 2])
                        gdone = inc("V", ve.scalar_tensor_tensor(
                            gstage[ot % 2][:, t2 * 512:(t2 + 1) * 512], ps_d[t2][:, :],
                            wsdp64[:, ot:ot + 1], sigd[t2][:, :], ALU.mult, ALU.mult))
                    wait(sy, "V", gdone)
                    g_spill[ot % 2] = inc("X", sy.dma_start(out=g_scr[ot],
                                                            in_=gstage[ot % 2][:, :]))
                if debug and th == 0:
                    wait(sy, "X")
                    inc("X", sy.dma_start(out=dbg["g0"][:, :], in_=g_scr[0]))
                wait(sy, "X")

        with ExitStack() as sc8:
            SB8 = lambda n, shp, dt: sc8.enter_context(nc.sbuf_tensor(n, shp, dt))
            PS8 = lambda n, shp, dt: sc8.enter_context(nc.psum_tensor(n, shp, dt))
            gcol = [SB8(f"gcol{th}_{i}", [128, NO * 128], BF16) for i in range(2)]
            gtok = [SB8(f"gtok{th}_{i}", [128, D], BF16) for i in range(2)]
            xbuf2 = [SB8(f"xbuf2{th}_{i}", [128, D], F32) for i in range(2)]
            sbuf2 = [SB8(f"sbuf2{th}_{i}", [128, D], BF16) for i in range(2)]
            sgb = SB8(f"sgb{th}", [128, D], F32)
            obuf = [SB8(f"obuf{th}_{i}", [128, D], F32) for i in range(2)]
            ps_t8 = [PS8(f"ps_t8{th}_{i}", [128, 128], BF16) for i in range(2)]
            ost = [0, 0]
            comb = [0, 0]
            gused = [0, 0]
            gev_bank = [0, 0]
            for ts in range(8):
                tt = th * 8 + ts
                wait(sy, "V", comb[ts % 2])
                wait(sy, "T", gused[ts % 2])
                gl = 0
                for ot in range(NO):
                    gl = inc("X", sy.dma_start(
                        out=gcol[ts % 2][:, ot * 128:(ot + 1) * 128],
                        in_=g_scr[ot][:, ts * 128:(ts + 1) * 128]))
                xl = inc("X", sy.dma_start(out=xbuf2[ts % 2][:, :],
                                           in_=xb[tt * 128:(tt + 1) * 128, :]))
                sl = inc("X", sy.dma_start(out=sbuf2[ts % 2][:, :], in_=s_scr[tt]))
                gev = 0
                for ot in range(NO):
                    wait(te, "X", gl)
                    wait(te, "V", gev_bank[ot % 2])
                    tg = inc("T", te.transpose(ps_t8[ot % 2][:, :],
                                               gcol[ts % 2][:, ot * 128:(ot + 1) * 128],
                                               ident[:, :]))
                    gused[ts % 2] = tg
                    wait(ve, "T", tg)
                    if ot == 0:
                        wait(ve, "X", ost[ts % 2])
                    gev = inc("V", ve.tensor_copy(
                        gtok[ts % 2][:, ot * 128:(ot + 1) * 128], ps_t8[ot % 2][:, :]))
                    gev_bank[ot % 2] = gev
                wait(ve, "X", max(xl, sl))
                ve.tensor_tensor(sgb[:, :], sbuf2[ts % 2][:, :], gtok[ts % 2][:, :],
                                 ALU.mult)
                comb[ts % 2] = inc("V", ve.tensor_tensor(
                    obuf[ts % 2][:, :], xbuf2[ts % 2][:, :], sgb[:, :], ALU.add))
                wait(sy, "V", comb[ts % 2])
                ost[ts % 2] = inc("X", sy.dma_start(
                    out=out[tt * 128:(tt + 1) * 128, :], in_=obuf[ts % 2][:, :]))
            wait(sy, "X", max(ost))
        nc.all_engine_barrier()

    ge.sem_clear(rr)
    nc.all_engine_barrier()
    ctx.close()
    return nc


def host_prep(inputs):
    wqkv = np.asarray(inputs["L0_attn_wqkv"], np.float32)
    wo = np.asarray(inputs["L0_attn_wo"], np.float32)
    wu = np.asarray(inputs["L0_ffn_W_up"], np.float32)
    wg = np.asarray(inputs["L0_ffn_W_gate"], np.float32)
    wd = np.asarray(inputs["L0_ffn_W_down"], np.float32)
    su_ug = np.asarray(inputs["L0_ffn_SU_ug"], np.float32)
    su_dp = np.asarray(inputs["L0_ffn_SU_dp"], np.float32)
    wsc_ug = np.asarray(inputs["L0_ffn_Wscale_ug"], np.float32)
    wsc_dp = np.asarray(inputs["L0_ffn_Wscale_dp"], np.float32)

    qk = wqkv[0:4096]
    wqkv_blk = np.ascontiguousarray(qk.reshape(32, 128, 16, 128).transpose(3, 0, 2, 1))
    wv = wqkv[4096:6144]
    wv_blk = np.ascontiguousarray(wv.reshape(4, 512, 16, 128).transpose(3, 0, 2, 1))
    wo_blk = np.ascontiguousarray(wo.T.reshape(16, 128, 2048).transpose(1, 0, 2))
    wu_blk = np.ascontiguousarray(wu.reshape(64, 128, 16, 128).transpose(3, 0, 2, 1))
    wg_blk = np.ascontiguousarray(wg.reshape(64, 128, 16, 128).transpose(3, 0, 2, 1))
    wd_blk = np.ascontiguousarray(wd.reshape(16, 128, 64, 128).transpose(3, 0, 2, 1))
    consts = build_host_consts()
    base = dict(
        wqkv_blk=wqkv_blk, wv_blk=wv_blk, wo_blk=wo_blk, wu_blk=wu_blk,
        wg_blk=wg_blk, wd_blk=wd_blk,
        su_ug_pt=np.ascontiguousarray(su_ug.reshape(16, 128).T),
        su_dp_pt=np.ascontiguousarray(su_dp.reshape(64, 128).T),
        wsu_pt=np.ascontiguousarray(wsc_ug[:INTER].reshape(64, 128).T),
        wsg_pt=np.ascontiguousarray(wsc_ug[INTER:].reshape(64, 128).T),
        wsdp_pt=np.ascontiguousarray(wsc_dp.reshape(16, 128).T),
        ident_in=consts["ident"], h128_in=consts["h128"], p2_in=consts["p2"],
        tri_in=consts["tri"], cos_in=consts["cosT"], sin_in=consts["sinT"])
    x = np.asarray(inputs["L0_attn_x"], np.float32)
    maps = []
    for b in range(8):
        m = dict(base)
        m["xb"] = np.ascontiguousarray(x[b])
        maps.append(m)
    return maps


def kernel(**inputs):
    maps = host_prep(inputs)
    nc = build_program(debug=False)
    res = run_bass_kernel_spmd(nc, maps, core_ids=list(range(8)))
    return np.stack([np.asarray(r["out"], np.float32) for r in res.results], axis=0)




# version 06
# speedup vs baseline: 1.1841x; 1.1841x over previous
import numpy as np
import ml_dtypes
from contextlib import ExitStack
import concourse.bass as bass
import concourse.mybir as mybir
from concourse.bass_utils import run_bass_kernel_spmd

F32 = mybir.dt.float32
BF16 = mybir.dt.bfloat16
AF = mybir.ActivationFunctionType
ALU = mybir.AluOpType
AX = mybir.AxisListType

L = 2048
D = 2048
NF = 16
H = 32
HD = 64
INTER = 8192
NI = 64
NO = 16
NT = 16
NQC = 4
NHALF = 2
TW = 1024
C2048 = float(1.0 / (64.0 * np.sqrt(2048.0)))
C8192 = float(1.0 / (64.0 * np.sqrt(8192.0)))
BF = ml_dtypes.bfloat16


def _sem_reset_range(nc):
    ksr = nc._kernel_sem_range
    start = ksr.start
    mono = start + (3 if nc._bir_kernel_barrier_sem is not None else 2)
    return range(mono + len(nc._monotonic_sems), ksr.stop)


def _hadamard(n):
    h = np.array([[1.0]], dtype=np.float32)
    while h.shape[0] < n:
        h = np.block([[h, h], [h, -h]])
    return h


def build_host_consts():
    ident = np.eye(128, dtype=BF)
    h128 = _hadamard(128).astype(BF)
    p2 = np.zeros((128, 128), dtype=np.float32)
    for d in range(128):
        e = (d % 64 + 32) % 64 + 64 * (d // 64)
        p2[e, d] = 1.0
    p2 = p2.astype(BF)
    ki, qi = np.meshgrid(np.arange(128), np.arange(128), indexing="ij")
    tri = (ki <= qi).astype(np.float32).astype(BF)
    inv_freq = 1.0 / 512.0 ** (np.arange(0, HD, 2, dtype=np.float32) / HD)
    t = np.arange(L, dtype=np.float32)
    didx = np.arange(128) % 64
    fr = t[None, :] * inv_freq[didx % 32][:, None]
    cosT = np.cos(fr).astype(BF)
    sgn = np.where(didx < 32, -1.0, 1.0).astype(np.float32)
    sinT = (np.sin(fr) * sgn[:, None]).astype(BF)
    return dict(ident=ident, h128=h128, p2=p2, tri=tri,
                cosT=np.ascontiguousarray(cosT), sinT=np.ascontiguousarray(sinT))


SER = 0

def build_program(debug=False):
    nc = bass.Bass("TRN2", target_bir_lowering=False)

    xb = nc.dram_tensor("xb", [L, D], F32, kind="ExternalInput").ap()
    wqkv_blk = nc.dram_tensor("wqkv_blk", [128, 32, NF, 128], F32, kind="ExternalInput").ap()
    wv_blk = nc.dram_tensor("wv_blk", [128, 4, NF, 512], F32, kind="ExternalInput").ap()
    wo_blk = nc.dram_tensor("wo_blk", [128, NF, D], F32, kind="ExternalInput").ap()
    wu_blk = nc.dram_tensor("wu_blk", [128, NI, NF, 128], F32, kind="ExternalInput").ap()
    wg_blk = nc.dram_tensor("wg_blk", [128, NI, NF, 128], F32, kind="ExternalInput").ap()
    wd_blk = nc.dram_tensor("wd_blk", [128, NO, NI, 128], F32, kind="ExternalInput").ap()
    su_ug_pt = nc.dram_tensor("su_ug_pt", [128, NF], F32, kind="ExternalInput").ap()
    su_dp_pt = nc.dram_tensor("su_dp_pt", [128, NI], F32, kind="ExternalInput").ap()
    wsu_pt = nc.dram_tensor("wsu_pt", [128, NI], F32, kind="ExternalInput").ap()
    wsg_pt = nc.dram_tensor("wsg_pt", [128, NI], F32, kind="ExternalInput").ap()
    wsdp_pt = nc.dram_tensor("wsdp_pt", [128, NO], F32, kind="ExternalInput").ap()
    ident_in = nc.dram_tensor("ident_in", [128, 128], BF16, kind="ExternalInput").ap()
    h128_in = nc.dram_tensor("h128_in", [128, 128], BF16, kind="ExternalInput").ap()
    p2_in = nc.dram_tensor("p2_in", [128, 128], BF16, kind="ExternalInput").ap()
    tri_in = nc.dram_tensor("tri_in", [128, 128], BF16, kind="ExternalInput").ap()
    cos_in = nc.dram_tensor("cos_in", [128, L], BF16, kind="ExternalInput").ap()
    sin_in = nc.dram_tensor("sin_in", [128, L], BF16, kind="ExternalInput").ap()
    out = nc.dram_tensor("out", [L, D], F32, kind="ExternalOutput").ap()

    xn_scr = nc.dram_tensor("xn_scr", [NF, 128, L], BF16, kind="Internal").ap()
    q_scr = nc.dram_tensor("q_scr", [NF, 128, L], BF16, kind="Internal").ap()
    v_scr = nc.dram_tensor("v_scr", [NT, 128, 32 * 65], BF16, kind="Internal").ap()
    at_scr = nc.dram_tensor("at_scr", [NF, 128, L], BF16, kind="Internal").ap()
    s_scr = nc.dram_tensor("s_scr", [NT, 128, D], BF16, kind="Internal").ap()
    g_scr = nc.dram_tensor("g_scr", [NO, 128, TW], BF16, kind="Internal").ap()

    dbg = {}
    if debug:
        for nm, shp in [("xnT0", [128, L]), ("kT0", [128, L]), ("q0", [128, L]),
                        ("v0", [128, 32 * 65]), ("att0", [128, L]), ("s0", [128, D]),
                        ("y0", [128, TW]), ("h0", [128, TW]), ("g0", [128, TW]),
                        ("att_all", [NF, 128, L]), ("s_all", [NT, 128, D]),
                        ("q_all", [NF, 128, L]), ("v_all", [NT, 128, 32 * 65]),
                        ("k_all", [NF, 128, L]), ("atok0", [128, D])]:
            dbg[nm] = nc.dram_tensor("dbg_" + nm, shp, BF16, kind="ExternalOutput").ap()

    te, ve, se, ge, sy = nc.tensor, nc.vector, nc.scalar, nc.gpsimd, nc.sync

    ctx = ExitStack()
    SB = lambda n, shp, dt: ctx.enter_context(nc.sbuf_tensor(n, shp, dt))
    sem = lambda n: ctx.enter_context(nc.semaphore(n))

    sW, sX, sT, sS, sV, sR = (sem(x) for x in ("sW", "sX", "sT", "sS", "sV", "sR"))
    cnt = {"W": 0, "X": 0, "T": 0, "S": 0, "V": 0, "R": 0}
    SEMS = {"W": sW, "X": sX, "T": sT, "S": sS, "V": sV, "R": sR}

    def inc(key, instr, n=None):
        n = (16 if key in ("W", "X") else 1) if n is None else n
        cnt[key] += n
        instr.then_inc(SEMS[key], n)
        return cnt[key]

    def wait(eng, key, val=None):
        v = int(cnt[key] if val is None else val)
        if v > 0:
            eng.wait_ge(SEMS[key], v)

    ident = SB("ident", [128, 128], BF16)
    h128c = SB("h128c", [128, 128], BF16)
    p2c = SB("p2c", [128, 128], BF16)
    tric = SB("tric", [128, 128], BF16)
    su_ug = SB("su_ug", [128, NF], F32)
    wsu64su = SB("wsu64su", [128, NI], F32)
    wsg64 = SB("wsg64", [128, NI], F32)
    wsdp64 = SB("wsdp64", [128, NO], F32)
    h128su = SB("h128su", [128, NF * 128], BF16)
    tmp_pc = SB("tmp_pc", [128, NI], F32)
    su_dp_sb = SB("su_dp_sb", [128, NI], F32)
    wsu_sb = SB("wsu_sb", [128, NI], F32)

    rr = _sem_reset_range(nc)
    ge.dma_reset(rr)
    ge.sem_clear(rr)
    nc.all_engine_barrier()

    for src, dst in [(ident_in, ident), (h128_in, h128c), (p2_in, p2c), (tri_in, tric)]:
        inc("X", sy.dma_start(out=dst[:, :], in_=src[:, :]))
    inc("X", sy.dma_start(out=su_ug[:, :], in_=su_ug_pt[:, :]))
    inc("X", sy.dma_start(out=su_dp_sb[:, :], in_=su_dp_pt[:, :]))
    inc("X", sy.dma_start(out=wsu_sb[:, :], in_=wsu_pt[:, :]))
    inc("X", sy.dma_start(out=wsg64[:, :], in_=wsg_pt[:, :]))
    inc("X", sy.dma_start(out=wsdp64[:, :], in_=wsdp_pt[:, :]))
    wait(ve, "X")
    e0_ = inc("R", ve.tensor_scalar(tmp_pc[:, :], wsu_sb[:, :], 64.0, None, ALU.mult))
    ve.wait_ge(sR, e0_)
    ve.tensor_tensor(wsu64su[:, :], tmp_pc[:, :], su_dp_sb[:, :], ALU.mult)
    ve.tensor_scalar(wsg64[:, :], wsg64[:, :], 64.0, None, ALU.mult)
    ve.tensor_scalar(wsdp64[:, :], wsdp64[:, :], 64.0, None, ALU.mult)
    for fc in range(NF):
        ve.tensor_scalar(h128su[:, fc * 128:(fc + 1) * 128], h128c[:, :],
                         su_ug[:, fc:fc + 1], None, ALU.mult)
    nc.all_engine_barrier()

    with ExitStack() as scB:
        kT = [scB.enter_context(nc.sbuf_tensor(f"kT{kd}", [128, L], BF16))
              for kd in range(NF)]

        with ExitStack() as scB1:
            xnT = [scB1.enter_context(nc.sbuf_tensor(f"xnT{fc}", [128, L], BF16))
                   for fc in range(NF)]

            with ExitStack() as sc1:
                SB1 = lambda n, shp, dt: sc1.enter_context(nc.sbuf_tensor(n, shp, dt))
                PS1 = lambda n, shp, dt: sc1.enter_context(nc.psum_tensor(n, shp, dt))
                xbuf = [SB1(f"xbuf{i}", [128, D], F32) for i in range(2)]
                sq_scr = SB1("sq_scr", [128, D], F32)
                s_sum = SB1("s_sum", [128, 4], F32)
                s_mu = SB1("s_mu", [128, 4], F32)
                s_mu2 = SB1("s_mu2", [128, 4], F32)
                s_bias = SB1("s_bias", [128, 4], F32)
                s_sq = SB1("s_sq", [128, 4], F32)
                s_rstd = SB1("s_rstd", [128, 4], F32)
                s_std = SB1("s_std", [128, 4], F32)
                xn_tok = SB1("xn_tok", [128, D], BF16)
                ps_tr = [PS1(f"ps_tr{i}", [128, 128], BF16) for i in range(2)]

                apply_done = [0, 0]
                evac_done = [0, 0]
                for tt in range(NT):
                    if tt >= 2:
                        wait(sy, "V", apply_done[tt % 2])
                    x_done = inc("X", sy.dma_start(out=xbuf[tt % 2][:, :],
                                                   in_=xb[tt * 128:(tt + 1) * 128, :]))
                    wait(ve, "X", x_done)
                    e1 = inc("R", ve.tensor_reduce(s_sum[:, 0:1], xbuf[tt % 2][:, :],
                                                   AX.X, ALU.add))
                    ve.wait_ge(sR, e1)
                    e2 = inc("R", ve.tensor_scalar(s_mu[:, 0:1], s_sum[:, 0:1],
                                                   1.0 / D, None, ALU.mult))
                    ve.wait_ge(sR, e2)
                    e3 = inc("R", ve.tensor_tensor(s_mu2[:, 0:1], s_mu[:, 0:1],
                                                   s_mu[:, 0:1], ALU.mult))
                    ve.wait_ge(sR, e3)
                    vb = inc("V", ve.tensor_scalar(s_bias[:, 0:1], s_mu2[:, 0:1],
                                                   -1.0, 1e-5, ALU.mult, ALU.add))
                    wait(se, "X", x_done)
                    se.activation(sq_scr[:, :], xbuf[tt % 2][:, :], AF.Square,
                                  accum_out=s_sq[:, 0:1])
                    wait(se, "V", vb)
                    sr = inc("S", se.activation(s_std[:, 0:1], s_sq[:, 0:1], AF.Sqrt,
                                                bias=s_bias[:, 0:1], scale=1.0 / D))
                    wait(ve, "S", sr)
                    rr_ = inc("R", ve.reciprocal(s_rstd[:, 0:1], s_std[:, 0:1]))
                    ve.wait_ge(sR, rr_)
                    va = inc("V", ve.tensor_scalar(xn_tok[:, :], xbuf[tt % 2][:, :],
                                                   s_mu[:, 0:1], s_rstd[:, 0:1],
                                                   ALU.subtract, ALU.mult))
                    apply_done[tt % 2] = va
                    wait(te, "V", va)
                    for j in range(NF):
                        wait(te, "V", evac_done[j % 2])
                        tj = inc("T", te.transpose(ps_tr[j % 2][:, :],
                                                   xn_tok[:, j * 128:(j + 1) * 128],
                                                   ident[:, :]))
                        wait(ve, "T", tj)
                        evac_done[j % 2] = inc("V", ve.tensor_copy(
                            xnT[j][:, tt * 128:(tt + 1) * 128], ps_tr[j % 2][:, :]))
                for fc in range(NF):
                    wait(sy, "V")
                    inc("X", sy.dma_start(out=xn_scr[fc], in_=xnT[fc][:, :]))
                if debug:
                    inc("X", sy.dma_start(out=dbg["xnT0"][:, :], in_=xnT[0][:, :]))
                wait(sy, "X")
            nc.all_engine_barrier()

            with ExitStack() as sc2:
                SB2 = lambda n, shp, dt: sc2.enter_context(nc.sbuf_tensor(n, shp, dt))
                PS2 = lambda n, shp, dt: sc2.enter_context(nc.psum_tensor(n, shp, dt))
                ones_t = SB2("ones_t", [128, 32 * 65], BF16)
                wvb = [SB2(f"wvb{i}", [128, NF * 256], BF16) for i in range(2)]
                vstage = [SB2(f"vstage{i}", [128, 256], BF16) for i in range(2)]
                ps_v = [PS2(f"ps_v{i}", [128, 256], F32) for i in range(2)]
                on_ = inc("V", ve.memset(ones_t[:, :], 1.0))
                wait(sy, "V", on_)
                for tt in range(NT):
                    inc("X", sy.dma_start(out=v_scr[tt], in_=ones_t[:, :]))
                NVC = 8
                wv_used = [0, 0]
                vst_dma = [0, 0]
                vst_evac = [0, 0]
                for ch in range(NVC):
                    vc, half = ch // 2, ch % 2
                    if ch >= 2:
                        wait(ge, "T", wv_used[ch % 2])
                    wvd = inc("W", ge.dma_start(
                        out=wvb[ch % 2][:, :].rearrange("p (f j) -> p f j", f=NF),
                        in_=wv_blk[:, vc, :, half * 256:(half + 1) * 256]))
                    for tt in range(NT):
                        wait(te, "W", wvd)
                        i2 = tt % 2
                        wait(te, "V", vst_evac[i2])
                        grp = None
                        for fc in range(NF):
                            grp = te.matmul(ps_v[i2][:, :],
                                            xnT[fc][:, tt * 128:(tt + 1) * 128],
                                            wvb[ch % 2][:, fc * 256:(fc + 1) * 256],
                                            start=(fc == 0), stop=(fc == NF - 1))
                        tg = inc("T", grp)
                        wv_used[ch % 2] = tg
                        wait(ve, "T", tg)
                        wait(ve, "X", vst_dma[i2])
                        ev = inc("V", ve.tensor_copy(vstage[i2][:, :], ps_v[i2][:, :]))
                        vst_evac[i2] = ev
                        wait(sy, "V", ev)
                        vst_dma[i2] = inc("X", sy.dma_start(
                            out=v_scr[tt].rearrange("p (h c) -> p h c", c=65)[
                                :, ch * 4:(ch + 1) * 4, 0:64],
                            in_=vstage[i2][:, :].rearrange("p (h c) -> p h c", c=64)))
                wait(sy, "X")
            nc.all_engine_barrier()

            with ExitStack() as scQ:
                SBQ = lambda n, shp, dt: scQ.enter_context(nc.sbuf_tensor(n, shp, dt))
                PSQ = lambda n, shp, dt: scQ.enter_context(nc.psum_tensor(n, shp, dt))
                cosT = SBQ("cosT", [128, L], BF16)
                sinT = SBQ("sinT", [128, L], BF16)
                wkb = [SBQ(f"wkb{i}", [128, D], BF16) for i in range(2)]
                raw = [SBQ(f"raw{i}", [128, 512], BF16) for i in range(2)]
                rot = [SBQ(f"rot{i}", [128, 512], BF16) for i in range(2)]
                tmpa = SBQ("tmpa", [128, 512], BF16)
                tmpb = SBQ("tmpb", [128, 512], BF16)
                qstage = [SBQ(f"qstage{i}", [128, L], BF16) for i in range(2)]
                ps_k = [PSQ(f"ps_k{i}", [128, 512], F32) for i in range(3)]
                ps_r = [PSQ(f"ps_r{i}", [128, 512], F32) for i in range(2)]
                inc("X", sy.dma_start(out=cosT[:, :], in_=cos_in[:, :]))
                inc("X", sy.dma_start(out=sinT[:, :], in_=sin_in[:, :]))
                rope_ready = cnt["X"]

                def qk_phase(is_q):
                    entry_T = cnt["T"]
                    wk_used = [entry_T, entry_T]
                    q_spill = [0, 0]
                    rope_done = [0, 0, 0]
                    raw_rd = [0, 0]
                    for dt_i in range(NF):
                        blk_idx = dt_i if is_q else (16 + dt_i)
                        wait(ge, "T", wk_used[dt_i % 2])
                        wd = inc("W", ge.dma_start(
                            out=wkb[dt_i % 2][:, :].rearrange("p (f j) -> p f j", f=NF),
                            in_=wqkv_blk[:, blk_idx]))
                        dstT = qstage[dt_i % 2] if is_q else kT[dt_i]
                        vv = 0
                        for tc in range(4):
                            wait(te, "W", wd)
                            wait(te, "V", rope_done[tc % 3])
                            grp = None
                            for fc in range(NF):
                                grp = te.matmul(ps_k[tc % 3][:, :],
                                                wkb[dt_i % 2][:, fc * 128:(fc + 1) * 128],
                                                xnT[fc][:, tc * 512:(tc + 1) * 512],
                                                start=(fc == 0), stop=(fc == NF - 1))
                            tg = inc("T", grp)
                            wk_used[dt_i % 2] = tg
                            wait(se, "T", tg)
                            if is_q:
                                wait(se, "X", q_spill[dt_i % 2])
                            if raw_rd[tc % 2]:
                                wait(se, "V", raw_rd[tc % 2])
                            sc = inc("S", se.activation(raw[tc % 2][:, :],
                                                        ps_k[tc % 3][:, :], AF.Copy))
                            wait(te, "S", sc)
                            tr = inc("T", te.matmul(ps_r[tc % 2][:, :], p2c[:, :],
                                                    raw[tc % 2][:, :],
                                                    start=True, stop=True))
                            wait(ve, "S", sc)
                            wait(ve, "X", rope_ready)
                            ve.tensor_tensor(tmpa[:, :], raw[tc % 2][:, :],
                                             cosT[:, tc * 512:(tc + 1) * 512], ALU.mult)
                            wait(ve, "T", tr)
                            ve.tensor_copy(rot[tc % 2][:, :], ps_r[tc % 2][:, :])
                            ve.tensor_tensor(tmpb[:, :], rot[tc % 2][:, :],
                                             sinT[:, tc * 512:(tc + 1) * 512], ALU.mult)
                            vv = inc("V", ve.tensor_tensor(
                                dstT[:, tc * 512:(tc + 1) * 512],
                                tmpa[:, :], tmpb[:, :], ALU.add))
                            rope_done[tc % 3] = vv
                            raw_rd[tc % 2] = vv
                        if is_q:
                            wait(sy, "V", vv)
                            q_spill[dt_i % 2] = inc("X", sy.dma_start(
                                out=q_scr[dt_i], in_=dstT[:, :]))
                    if is_q:
                        wait(sy, "X")

                qk_phase(False)
                qk_phase(True)
                if debug:
                    wait(sy, "V")
                    inc("X", sy.dma_start(out=dbg["kT0"][:, :], in_=kT[0][:, :]))
                    for i_ in range(NF):
                        inc("X", sy.dma_start(out=dbg["k_all"][i_], in_=kT[i_][:, :]))
            nc.all_engine_barrier()

        with ExitStack() as sc3:
            SB3 = lambda n, shp, dt: sc3.enter_context(nc.sbuf_tensor(n, shp, dt))
            PS3 = lambda n, shp, dt: sc3.enter_context(nc.psum_tensor(n, shp, dt))
            v_aug = [SB3(f"vaug{tt}", [128, 32 * 65], BF16) for tt in range(NT)]
            qbuf = [SB3(f"qbuf{qd}", [128, 512], BF16) for qd in range(NF)]
            probs = [[SB3(f"probs{pp}_{kt}", [128, 512], BF16) for kt in range(NT)]
                     for pp in range(2)]
            attn_tok = [SB3(f"attn_tok{qs}", [128, D], BF16) for qs in range(4)]
            atb = [SB3(f"atb{i}", [128, 128], BF16) for i in range(2)]
            rec = SB3("rec", [128, 16], F32)
            ps_s = [PS3(f"ps_s{i}", [128, 512], F32) for i in range(3)]
            ps_av = [PS3(f"ps_av{i}", [128, 65], F32) for i in range(3)]
            ps_tr3 = [PS3(f"ps_tr3{i}", [128, 128], BF16) for i in range(2)]

            for tt in range(NT):
                inc("X", sy.dma_start(out=v_aug[tt][:, :], in_=v_scr[tt]))
            v_ready = cnt["X"]

            exp_done = np.zeros((2, NT), dtype=np.int64)
            exp_bank = [0, 0, 0]
            mask_done = np.zeros((2, NT), dtype=np.int64)
            av_last_user = np.zeros((2, NT), dtype=np.int64)
            norm_done = [0, 0, 0]
            atb_dma = [0, 0]
            first = True
            for qc in range(NQC):
                if qc >= 1:
                    wait(sy, "T")
                qld = [0] * NF
                for qd in range(NF):
                    qld[qd] = inc("X", sy.dma_start(
                        out=qbuf[qd][:, :], in_=q_scr[qd][:, qc * 512:(qc + 1) * 512]))
                nkt = 4 * qc + 4
                hseq = (list(range(H)) if qc == 0 else []) + list(range(H))
                for h_i in hseq:
                    pp = h_i % 2
                    kd, off = h_i // 2, (h_i % 2) * 64
                    for kt in range(nkt):
                        jq0 = max(0, (kt - 4 * qc)) * 128
                        wait(te, "X", qld[kd])
                        if first:
                            wait(te, "X", v_ready)
                            first = False
                        wait(te, "S", exp_bank[kt % 3])
                        tg = inc("T", te.matmul(
                            ps_s[kt % 3][:, jq0:512],
                            kT[kd][off:off + 64, kt * 128:(kt + 1) * 128],
                            qbuf[kd][off:off + 64, jq0:512], start=True, stop=True))
                        wait(se, "T", tg)
                        wait(se, "T", av_last_user[pp][kt])
                        ex = inc("S", se.activation(probs[pp][kt][:, jq0:512],
                                                    ps_s[kt % 3][:, jq0:512],
                                                    AF.Exp, scale=0.125))
                        exp_done[pp][kt] = ex
                        exp_bank[kt % 3] = ex
                        if kt >= 4 * qc:
                            j = kt - 4 * qc
                            wait(ve, "S", ex)
                            mask_done[pp][kt] = inc("V", ve.tensor_tensor(
                                probs[pp][kt][:, j * 128:(j + 1) * 128],
                                probs[pp][kt][:, j * 128:(j + 1) * 128],
                                tric[:, :], ALU.mult))
                    rr_ = 0
                    for qs in range(4):
                        qs128 = qc * 4 + qs
                        wait(te, "S", exp_done[pp][qs128])
                        wait(te, "V", mask_done[pp][qs128])
                        wait(te, "V", norm_done[qs % 3])
                        grp = None
                        for kt in range(qs128 + 1):
                            grp = te.matmul(ps_av[qs % 3][:, :],
                                            probs[pp][kt][:, qs * 128:(qs + 1) * 128],
                                            v_aug[kt][:, h_i * 65:h_i * 65 + 65],
                                            start=(kt == 0), stop=(kt == qs128))
                        avg = inc("T", grp)
                        for kt in range(qs128 + 1):
                            av_last_user[pp][kt] = avg
                        wait(ve, "T", avg)
                        rr_ = inc("R", ve.reciprocal(rec[:, qs * 4:qs * 4 + 1],
                                                     ps_av[qs % 3][:, 64:65]))
                        if qs >= 1:
                            norm_done[(qs - 1) % 3] = inc("V", ve.tensor_scalar(
                                attn_tok[qs - 1][:, h_i * 64:(h_i + 1) * 64],
                                ps_av[(qs - 1) % 3][:, 0:64],
                                rec[:, (qs - 1) * 4:(qs - 1) * 4 + 1], None, ALU.mult))
                    ve.wait_ge(sR, rr_)
                    norm_done[3 % 3] = inc("V", ve.tensor_scalar(
                        attn_tok[3][:, h_i * 64:(h_i + 1) * 64],
                        ps_av[3 % 3][:, 0:64],
                        rec[:, 12:13], None, ALU.mult))
                if debug and qc == 0:
                    wait(sy, "V")
                    inc("X", sy.dma_start(out=dbg["atok0"][:, :], in_=attn_tok[0][:, :]))
                    wait(sy, "X")
                wait(te, "V")
                tr_evac = [0, 0]
                for qs in range(4):
                    for dt_i in range(NF):
                        i2 = dt_i % 2
                        wait(te, "V", tr_evac[i2])
                        tg = inc("T", te.transpose(
                            ps_tr3[i2][:, :],
                            attn_tok[qs][:, dt_i * 128:(dt_i + 1) * 128], ident[:, :]))
                        wait(ve, "T", tg)
                        wait(ve, "X", atb_dma[i2])
                        tr_evac[i2] = inc("V", ve.tensor_copy(atb[i2][:, :],
                                                              ps_tr3[i2][:, :]))
                        wait(sy, "V", tr_evac[i2])
                        atb_dma[i2] = inc("X", sy.dma_start(
                            out=at_scr[dt_i][:, qc * 512 + qs * 128:
                                             qc * 512 + (qs + 1) * 128],
                            in_=atb[i2][:, :]))
            wait(sy, "X")
            if debug:
                inc("X", sy.dma_start(out=dbg["att0"][:, :], in_=at_scr[0]))
                inc("X", sy.dma_start(out=dbg["q0"][:, :], in_=q_scr[0]))
                inc("X", sy.dma_start(out=dbg["v0"][:, :], in_=v_scr[0]))
                for i_ in range(NF):
                    inc("X", sy.dma_start(out=dbg["att_all"][i_], in_=at_scr[i_]))
                    inc("X", sy.dma_start(out=dbg["q_all"][i_], in_=q_scr[i_]))
                for i_ in range(NT):
                    inc("X", sy.dma_start(out=dbg["v_all"][i_], in_=v_scr[i_]))
        nc.all_engine_barrier()

    with ExitStack() as sc4:
        SB4 = lambda n, shp, dt: sc4.enter_context(nc.sbuf_tensor(n, shp, dt))
        PS4 = lambda n, shp, dt: sc4.enter_context(nc.psum_tensor(n, shp, dt))
        woT = SB4("woT", [128, NF * 2048], BF16)
        attb = [SB4(f"attb{dt_i}", [128, 512], BF16) for dt_i in range(NF)]
        s_stage = [SB4(f"s_stage{i}", [128, D], BF16) for i in range(2)]
        ps_wo = [PS4(f"ps_wo{i}", [128, 512], F32) for i in range(4)]
        wo_done = 0
        for g4 in range(4):
            wo_done = inc("W", ge.dma_start(
                out=woT[:, g4 * 4 * 2048:(g4 + 1) * 4 * 2048].rearrange(
                    "p (f j) -> p f j", f=4),
                in_=wo_blk[:, g4 * 4:(g4 + 1) * 4]))
        sp_done = [0, 0]
        ev_bank = [0, 0, 0, 0]
        for qc in range(NQC):
            if qc >= 1:
                wait(sy, "T")
            for dt_i in range(NF):
                inc("X", sy.dma_start(out=attb[dt_i][:, :],
                                      in_=at_scr[dt_i][:, qc * 512:(qc + 1) * 512]))
            att_ready = cnt["X"]
            for qs in range(4):
                ev = 0
                for oc in range(4):
                    wait(te, "W", wo_done)
                    wait(te, "X", att_ready)
                    wait(te, "S", ev_bank[oc])
                    grp = None
                    for fc in range(NF):
                        grp = te.matmul(
                            ps_wo[oc][:, :],
                            attb[fc][:, qs * 128:(qs + 1) * 128],
                            woT[:, fc * 2048 + oc * 512:fc * 2048 + (oc + 1) * 512],
                            start=(fc == 0), stop=(fc == NF - 1))
                    tg = inc("T", grp)
                    wait(se, "T", tg)
                    if oc == 0:
                        wait(se, "X", sp_done[qs % 2])
                    ev = inc("S", se.activation(
                        s_stage[qs % 2][:, oc * 512:(oc + 1) * 512],
                        ps_wo[oc][:, :], AF.Copy))
                    ev_bank[oc] = ev
                wait(sy, "S", ev)
                sp_done[qs % 2] = inc("X", sy.dma_start(out=s_scr[qc * 4 + qs],
                                                        in_=s_stage[qs % 2][:, :]))
        wait(sy, "X")
        if debug:
            inc("X", sy.dma_start(out=dbg["s0"][:, :], in_=s_scr[0]))
            for i_ in range(NT):
                inc("X", sy.dma_start(out=dbg["s_all"][i_], in_=s_scr[i_]))
    nc.all_engine_barrier()

    for th in range(NHALF):
        tb = th * TW
        with ExitStack() as sc5h:
            SBH = lambda n, shp, dt: sc5h.enter_context(nc.sbuf_tensor(n, shp, dt))
            hbuf = [SBH(f"h{th}_{it}", [128, TW], BF16) for it in range(NI)]

            with ExitStack() as sc6:
                SB6 = lambda n, shp, dt: sc6.enter_context(nc.sbuf_tensor(n, shp, dt))
                PS6 = lambda n, shp, dt: sc6.enter_context(nc.psum_tensor(n, shp, dt))
                ybuf = [SB6(f"y{th}_{fc}", [128, TW], BF16) for fc in range(NF)]
                xnb = [SB6(f"xnb{th}_{i}", [128, TW], BF16) for i in range(2)]
                wub = [SB6(f"wub{th}_{i}", [128, D], BF16) for i in range(2)]
                wgb = [SB6(f"wgb{th}_{i}", [128, D], BF16) for i in range(2)]
                sigb = [SB6(f"sigb{th}_{i}", [128, 512], F32) for i in range(2)]
                t1b = [SB6(f"t1b{th}_{i}", [128, 512], F32) for i in range(2)]
                ps_u = [PS6(f"ps_u{th}_{i}", [128, 512], F32) for i in range(4)]
                ps_g = [PS6(f"ps_g{th}_{i}", [128, 512], F32) for i in range(4)]

                xn_used = [0, 0]
                yev_bank = [0, 0, 0, 0]
                yev = 0
                for fc in range(NF):
                    if fc >= 2:
                        wait(sy, "T", xn_used[fc % 2])
                    xld = inc("X", sy.dma_start(out=xnb[fc % 2][:, :],
                                                in_=xn_scr[fc][:, tb:tb + TW]))
                    for t2 in range(2):
                        bk = (fc * 2 + t2) % 4
                        wait(te, "X", xld)
                        wait(te, "S", yev_bank[bk])
                        tg = inc("T", te.matmul(ps_u[bk][:, :],
                                                h128su[:, fc * 128:(fc + 1) * 128],
                                                xnb[fc % 2][:, t2 * 512:(t2 + 1) * 512],
                                                start=True, stop=True))
                        xn_used[fc % 2] = tg
                        wait(se, "T", tg)
                        yev = inc("S", se.activation(
                            ybuf[fc][:, t2 * 512:(t2 + 1) * 512],
                            ps_u[bk][:, :], AF.Copy, scale=C2048))
                        yev_bank[bk] = yev
                y_ready_t2 = [0, 0]
                wait(ve, "S", yev)
                for t2c in range(2):
                    c0, c1 = t2c * 512, (t2c + 1) * 512
                    hh = 1
                    while hh < NF:
                        for base in range(0, NF, 2 * hh):
                            for j in range(hh):
                                a, b = ybuf[base + j], ybuf[base + hh + j]
                                ve.tensor_tensor(a[:, c0:c1], a[:, c0:c1], b[:, c0:c1],
                                                 ALU.add)
                                ve.scalar_tensor_tensor(b[:, c0:c1], b[:, c0:c1], -2.0,
                                                        a[:, c0:c1], ALU.mult, ALU.add)
                        hh *= 2
                    y_ready_t2[t2c] = inc("V", ve.tensor_copy(tmp_pc[:, 0:1],
                                                              tmp_pc[:, 0:1]))
                if debug and th == 0:
                    wait(sy, "V", y_ready_t2[1])
                    inc("X", sy.dma_start(out=dbg["y0"][:, :], in_=ybuf[0][:, :]))

                wug_used = [0, 0]
                hdone_bank = [0, 0, 0, 0]
                sig_rd = [0, 0]
                hdone = 0
                for it in range(NI):
                    if it >= 2:
                        wait(ge, "T", wug_used[it % 2])
                    wu = inc("W", ge.dma_start(out=wub[it % 2][:, :].rearrange(
                        "p (f j) -> p f j", f=NF), in_=wu_blk[:, it]))
                    wg = inc("W", ge.dma_start(out=wgb[it % 2][:, :].rearrange(
                        "p (f j) -> p f j", f=NF), in_=wg_blk[:, it]))
                    for t2 in range(2):
                        bk = (it * 2 + t2) % 4
                        wait(te, "W", wg)
                        wait(te, "V", y_ready_t2[t2])
                        wait(te, "V", hdone_bank[bk])
                        for fc in range(NF):
                            te.matmul(ps_u[bk][:, :],
                                      wub[it % 2][:, fc * 128:(fc + 1) * 128],
                                      ybuf[fc][:, t2 * 512:(t2 + 1) * 512],
                                      start=(fc == 0), stop=(fc == NF - 1))
                        gg = None
                        for fc in range(NF):
                            gg = te.matmul(ps_g[bk][:, :],
                                           wgb[it % 2][:, fc * 128:(fc + 1) * 128],
                                           ybuf[fc][:, t2 * 512:(t2 + 1) * 512],
                                           start=(fc == 0), stop=(fc == NF - 1))
                        tg = inc("T", gg)
                        wug_used[it % 2] = tg
                        wait(se, "T", tg)
                        wait(se, "V", sig_rd[t2])
                        sg_ = inc("S", se.activation(sigb[t2][:, :], ps_g[bk][:, :],
                                                     AF.Sigmoid,
                                                     scale=wsg64[:, it:it + 1]))
                        wait(ve, "S", sg_)
                        ve.scalar_tensor_tensor(t1b[t2][:, :], ps_g[bk][:, :],
                                                wsg64[:, it:it + 1], sigb[t2][:, :],
                                                ALU.mult, ALU.mult)
                        hdone = inc("V", ve.scalar_tensor_tensor(
                            hbuf[it][:, t2 * 512:(t2 + 1) * 512], ps_u[bk][:, :],
                            wsu64su[:, it:it + 1], t1b[t2][:, :], ALU.mult, ALU.mult))
                        hdone_bank[bk] = hdone
                        sig_rd[t2] = hdone

                hev_bank = [0, 0, 0, 0]
                hev_t2 = [0, 0]
                h_ready_t2 = [0, 0]
                hev = 0
                wait(te, "V", hdone)
                for t2 in range(2):
                    for it in range(NI):
                        bk = (t2 * NI + it) % 4
                        wait(te, "S", hev_bank[bk])
                        if t2 == 1 and it < 4:
                            pass
                        tg = inc("T", te.matmul(ps_u[bk][:, :], h128c[:, :],
                                                hbuf[it][:, t2 * 512:(t2 + 1) * 512],
                                                start=True, stop=True))
                        wait(se, "T", tg)
                        hev = inc("S", se.activation(
                            hbuf[it][:, t2 * 512:(t2 + 1) * 512],
                            ps_u[bk][:, :], AF.Copy, scale=C8192))
                        hev_bank[bk] = hev
                    hev_t2[t2] = hev
                    c0, c1 = t2 * 512, (t2 + 1) * 512
                    wait(ve, "S", hev_t2[t2])
                    hh = 1
                    while hh < NI:
                        for base in range(0, NI, 2 * hh):
                            for j in range(hh):
                                a, b = hbuf[base + j], hbuf[base + hh + j]
                                ve.tensor_tensor(a[:, c0:c1], a[:, c0:c1], b[:, c0:c1],
                                                 ALU.add)
                                ve.scalar_tensor_tensor(b[:, c0:c1], b[:, c0:c1], -2.0,
                                                        a[:, c0:c1], ALU.mult, ALU.add)
                        hh *= 2
                    h_ready_t2[t2] = inc("V", ve.tensor_copy(tmp_pc[:, 0:1],
                                                             tmp_pc[:, 0:1]))
                if debug and th == 0:
                    wait(sy, "V", h_ready_t2[1])
                    inc("X", sy.dma_start(out=dbg["h0"][:, :], in_=hbuf[0][:, :]))

            with ExitStack() as sc7:
                SB7 = lambda n, shp, dt: sc7.enter_context(nc.sbuf_tensor(n, shp, dt))
                PS7 = lambda n, shp, dt: sc7.enter_context(nc.psum_tensor(n, shp, dt))
                wdb = [SB7(f"wdb{th}_{i}", [128, 32 * 128], BF16) for i in range(2)]
                gstage = [SB7(f"gstage{th}_{i}", [128, TW], BF16) for i in range(2)]
                sigd = [SB7(f"sigd{th}_{i}", [128, 512], F32) for i in range(4)]
                ps_d = [PS7(f"ps_d{th}_{i}", [128, 512], F32) for i in range(8)]
                wait(ge, "T")
                gdone_bank = [0] * 8
                wd_used = [0, 0]
                sigd_rd = [0, 0, 0, 0]
                for t2 in range(2):
                    g_spill2 = [0, 0]
                    for ot in range(NO):
                        bk = (t2 * 16 + ot) % 8
                        wd_done = [0, 0]
                        for hf in range(2):
                            if t2 + ot > 0:
                                wait(ge, "T", wd_used[hf])
                            wd_done[hf] = inc("W", ge.dma_start(
                                out=wdb[hf][:, :].rearrange("p (i j) -> p i j", i=32),
                                in_=wd_blk[:, ot, hf * 32:(hf + 1) * 32]))
                        wait(te, "V", h_ready_t2[t2])
                        wait(te, "V", gdone_bank[bk])
                        grp = None
                        for it in range(NI):
                            hf, il = it // 32, it % 32
                            if il == 0:
                                wait(te, "W", wd_done[hf])
                            grp = te.matmul(ps_d[bk][:, :],
                                            wdb[hf][:, il * 128:(il + 1) * 128],
                                            hbuf[it][:, t2 * 512:(t2 + 1) * 512],
                                            start=(it == 0), stop=(it == NI - 1))
                        tg = inc("T", grp)
                        wd_used[0] = wd_used[1] = tg
                        wait(se, "T", tg)
                        wait(se, "V", sigd_rd[ot % 4])
                        sd = inc("S", se.activation(sigd[ot % 4][:, :], ps_d[bk][:, :],
                                                    AF.Sigmoid,
                                                    scale=wsdp64[:, ot:ot + 1]))
                        wait(ve, "S", sd)
                        wait(ve, "X", g_spill2[ot % 2])
                        gdone = inc("V", ve.scalar_tensor_tensor(
                            gstage[ot % 2][:, t2 * 512:(t2 + 1) * 512], ps_d[bk][:, :],
                            wsdp64[:, ot:ot + 1], sigd[ot % 4][:, :],
                            ALU.mult, ALU.mult))
                        gdone_bank[bk] = gdone
                        sigd_rd[ot % 4] = gdone
                        wait(sy, "V", gdone)
                        g_spill2[ot % 2] = inc("X", sy.dma_start(
                            out=g_scr[ot][:, t2 * 512:(t2 + 1) * 512],
                            in_=gstage[ot % 2][:, t2 * 512:(t2 + 1) * 512]))
                if debug and th == 0:
                    wait(sy, "X")
                    inc("X", sy.dma_start(out=dbg["g0"][:, :], in_=g_scr[0]))
                wait(sy, "X")

        with ExitStack() as sc8:
            SB8 = lambda n, shp, dt: sc8.enter_context(nc.sbuf_tensor(n, shp, dt))
            PS8 = lambda n, shp, dt: sc8.enter_context(nc.psum_tensor(n, shp, dt))
            gcol = [SB8(f"gcol{th}_{i}", [128, NO * 128], BF16) for i in range(2)]
            gtok = [SB8(f"gtok{th}_{i}", [128, D], BF16) for i in range(2)]
            xbuf2 = [SB8(f"xbuf2{th}_{i}", [128, D], F32) for i in range(2)]
            sbuf2 = [SB8(f"sbuf2{th}_{i}", [128, D], BF16) for i in range(2)]
            sgb = SB8(f"sgb{th}", [128, D], F32)
            obuf = [SB8(f"obuf{th}_{i}", [128, D], F32) for i in range(2)]
            ps_t8 = [PS8(f"ps_t8{th}_{i}", [128, 128], BF16) for i in range(4)]
            ost = [0, 0]
            comb = [0, 0]
            gused = [0, 0]
            gev_bank = [0, 0, 0, 0]
            for ts in range(8):
                tt = th * 8 + ts
                wait(sy, "V", comb[ts % 2])
                wait(sy, "T", gused[ts % 2])
                gl = 0
                for ot in range(NO):
                    gl = inc("X", sy.dma_start(
                        out=gcol[ts % 2][:, ot * 128:(ot + 1) * 128],
                        in_=g_scr[ot][:, ts * 128:(ts + 1) * 128]))
                xl = inc("X", sy.dma_start(out=xbuf2[ts % 2][:, :],
                                           in_=xb[tt * 128:(tt + 1) * 128, :]))
                sl = inc("X", sy.dma_start(out=sbuf2[ts % 2][:, :], in_=s_scr[tt]))
                gev = 0
                for ot in range(NO):
                    wait(te, "X", gl)
                    wait(te, "V", gev_bank[ot % 4])
                    tg = inc("T", te.transpose(ps_t8[ot % 4][:, :],
                                               gcol[ts % 2][:, ot * 128:(ot + 1) * 128],
                                               ident[:, :]))
                    gused[ts % 2] = tg
                    wait(ve, "T", tg)
                    if ot == 0:
                        wait(ve, "X", ost[ts % 2])
                    gev = inc("V", ve.tensor_copy(
                        gtok[ts % 2][:, ot * 128:(ot + 1) * 128], ps_t8[ot % 4][:, :]))
                    gev_bank[ot % 4] = gev
                wait(ve, "X", max(xl, sl))
                ve.tensor_tensor(sgb[:, :], sbuf2[ts % 2][:, :], gtok[ts % 2][:, :],
                                 ALU.mult)
                comb[ts % 2] = inc("V", ve.tensor_tensor(
                    obuf[ts % 2][:, :], xbuf2[ts % 2][:, :], sgb[:, :], ALU.add))
                wait(sy, "V", comb[ts % 2])
                ost[ts % 2] = inc("X", sy.dma_start(
                    out=out[tt * 128:(tt + 1) * 128, :], in_=obuf[ts % 2][:, :]))
            wait(sy, "X", max(ost))
        nc.all_engine_barrier()

    ge.sem_clear(rr)
    nc.all_engine_barrier()
    ctx.close()
    return nc


def host_prep(inputs):
    wqkv = np.asarray(inputs["L0_attn_wqkv"], np.float32)
    wo = np.asarray(inputs["L0_attn_wo"], np.float32)
    wu = np.asarray(inputs["L0_ffn_W_up"], np.float32)
    wg = np.asarray(inputs["L0_ffn_W_gate"], np.float32)
    wd = np.asarray(inputs["L0_ffn_W_down"], np.float32)
    su_ug = np.asarray(inputs["L0_ffn_SU_ug"], np.float32)
    su_dp = np.asarray(inputs["L0_ffn_SU_dp"], np.float32)
    wsc_ug = np.asarray(inputs["L0_ffn_Wscale_ug"], np.float32)
    wsc_dp = np.asarray(inputs["L0_ffn_Wscale_dp"], np.float32)

    qk = wqkv[0:4096]
    wqkv_blk = np.ascontiguousarray(qk.reshape(32, 128, 16, 128).transpose(3, 0, 2, 1))
    wv = wqkv[4096:6144]
    wv_blk = np.ascontiguousarray(wv.reshape(4, 512, 16, 128).transpose(3, 0, 2, 1))
    wo_blk = np.ascontiguousarray(wo.T.reshape(16, 128, 2048).transpose(1, 0, 2))
    wu_blk = np.ascontiguousarray(wu.reshape(64, 128, 16, 128).transpose(3, 0, 2, 1))
    wg_blk = np.ascontiguousarray(wg.reshape(64, 128, 16, 128).transpose(3, 0, 2, 1))
    wd_blk = np.ascontiguousarray(wd.reshape(16, 128, 64, 128).transpose(3, 0, 2, 1))
    consts = build_host_consts()
    base = dict(
        wqkv_blk=wqkv_blk, wv_blk=wv_blk, wo_blk=wo_blk, wu_blk=wu_blk,
        wg_blk=wg_blk, wd_blk=wd_blk,
        su_ug_pt=np.ascontiguousarray(su_ug.reshape(16, 128).T),
        su_dp_pt=np.ascontiguousarray(su_dp.reshape(64, 128).T),
        wsu_pt=np.ascontiguousarray(wsc_ug[:INTER].reshape(64, 128).T),
        wsg_pt=np.ascontiguousarray(wsc_ug[INTER:].reshape(64, 128).T),
        wsdp_pt=np.ascontiguousarray(wsc_dp.reshape(16, 128).T),
        ident_in=consts["ident"], h128_in=consts["h128"], p2_in=consts["p2"],
        tri_in=consts["tri"], cos_in=consts["cosT"], sin_in=consts["sinT"])
    x = np.asarray(inputs["L0_attn_x"], np.float32)
    maps = []
    for b in range(8):
        m = dict(base)
        m["xb"] = np.ascontiguousarray(x[b])
        maps.append(m)
    return maps


def kernel(**inputs):
    maps = host_prep(inputs)
    nc = build_program(debug=False)
    res = run_bass_kernel_spmd(nc, maps, core_ids=list(range(8)))
    return np.stack([np.asarray(r["out"], np.float32) for r in res.results], axis=0)




# version 07
# speedup vs baseline: 1.2077x; 1.0200x over previous
import numpy as np
import ml_dtypes
from contextlib import ExitStack
import concourse.bass as bass
import concourse.mybir as mybir
from concourse.bass_utils import run_bass_kernel_spmd

F32 = mybir.dt.float32
BF16 = mybir.dt.bfloat16
AF = mybir.ActivationFunctionType
ALU = mybir.AluOpType
AX = mybir.AxisListType

L = 2048
D = 2048
NF = 16
H = 32
HD = 64
INTER = 8192
NI = 64
NO = 16
NT = 16
NQC = 4
NHALF = 2
TW = 1024
C2048 = float(1.0 / (64.0 * np.sqrt(2048.0)))
C8192 = float(1.0 / (64.0 * np.sqrt(8192.0)))
BF = ml_dtypes.bfloat16


def _sem_reset_range(nc):
    ksr = nc._kernel_sem_range
    start = ksr.start
    mono = start + (3 if nc._bir_kernel_barrier_sem is not None else 2)
    return range(mono + len(nc._monotonic_sems), ksr.stop)


def _hadamard(n):
    h = np.array([[1.0]], dtype=np.float32)
    while h.shape[0] < n:
        h = np.block([[h, h], [h, -h]])
    return h


def build_host_consts():
    ident = np.eye(128, dtype=BF)
    h128 = _hadamard(128).astype(BF)
    p2 = np.zeros((128, 128), dtype=np.float32)
    for d in range(128):
        e = (d % 64 + 32) % 64 + 64 * (d // 64)
        p2[e, d] = 1.0
    p2 = p2.astype(BF)
    ki, qi = np.meshgrid(np.arange(128), np.arange(128), indexing="ij")
    tri = (ki <= qi).astype(np.float32).astype(BF)
    inv_freq = 1.0 / 512.0 ** (np.arange(0, HD, 2, dtype=np.float32) / HD)
    t = np.arange(L, dtype=np.float32)
    didx = np.arange(128) % 64
    fr = t[None, :] * inv_freq[didx % 32][:, None]
    cosT = np.cos(fr).astype(BF)
    sgn = np.where(didx < 32, -1.0, 1.0).astype(np.float32)
    sinT = (np.sin(fr) * sgn[:, None]).astype(BF)
    return dict(ident=ident, h128=h128, p2=p2, tri=tri,
                cosT=np.ascontiguousarray(cosT), sinT=np.ascontiguousarray(sinT))


SER = 0

def build_program(debug=False):
    nc = bass.Bass("TRN2", target_bir_lowering=False)

    xb = nc.dram_tensor("xb", [L, D], F32, kind="ExternalInput").ap()
    wqkv_blk = nc.dram_tensor("wqkv_blk", [128, 32, NF, 128], F32, kind="ExternalInput").ap()
    wv_blk = nc.dram_tensor("wv_blk", [128, 4, NF, 512], F32, kind="ExternalInput").ap()
    wo_blk = nc.dram_tensor("wo_blk", [128, NF, D], F32, kind="ExternalInput").ap()
    wu_blk = nc.dram_tensor("wu_blk", [128, NI, NF, 128], F32, kind="ExternalInput").ap()
    wg_blk = nc.dram_tensor("wg_blk", [128, NI, NF, 128], F32, kind="ExternalInput").ap()
    wd_blk = nc.dram_tensor("wd_blk", [128, NO, NI, 128], F32, kind="ExternalInput").ap()
    su_ug_pt = nc.dram_tensor("su_ug_pt", [128, NF], F32, kind="ExternalInput").ap()
    su_dp_pt = nc.dram_tensor("su_dp_pt", [128, NI], F32, kind="ExternalInput").ap()
    wsu_pt = nc.dram_tensor("wsu_pt", [128, NI], F32, kind="ExternalInput").ap()
    wsg_pt = nc.dram_tensor("wsg_pt", [128, NI], F32, kind="ExternalInput").ap()
    wsdp_pt = nc.dram_tensor("wsdp_pt", [128, NO], F32, kind="ExternalInput").ap()
    ident_in = nc.dram_tensor("ident_in", [128, 128], BF16, kind="ExternalInput").ap()
    h128_in = nc.dram_tensor("h128_in", [128, 128], BF16, kind="ExternalInput").ap()
    p2_in = nc.dram_tensor("p2_in", [128, 128], BF16, kind="ExternalInput").ap()
    tri_in = nc.dram_tensor("tri_in", [128, 128], BF16, kind="ExternalInput").ap()
    cos_in = nc.dram_tensor("cos_in", [128, L], BF16, kind="ExternalInput").ap()
    sin_in = nc.dram_tensor("sin_in", [128, L], BF16, kind="ExternalInput").ap()
    out = nc.dram_tensor("out", [L, D], F32, kind="ExternalOutput").ap()

    xn_scr = nc.dram_tensor("xn_scr", [NF, 128, L], BF16, kind="Internal").ap()
    q_scr = nc.dram_tensor("q_scr", [NF, 128, L], BF16, kind="Internal").ap()
    v_scr = nc.dram_tensor("v_scr", [NT, 128, 32 * 65], BF16, kind="Internal").ap()
    at_scr = nc.dram_tensor("at_scr", [NF, 128, L], BF16, kind="Internal").ap()
    s_scr = nc.dram_tensor("s_scr", [NT, 128, D], BF16, kind="Internal").ap()
    g_scr = nc.dram_tensor("g_scr", [NO, 128, TW], BF16, kind="Internal").ap()

    dbg = {}
    if debug:
        for nm, shp in [("xnT0", [128, L]), ("kT0", [128, L]), ("q0", [128, L]),
                        ("v0", [128, 32 * 65]), ("att0", [128, L]), ("s0", [128, D]),
                        ("y0", [128, TW]), ("h0", [128, TW]), ("g0", [128, TW]),
                        ("att_all", [NF, 128, L]), ("s_all", [NT, 128, D]),
                        ("q_all", [NF, 128, L]), ("v_all", [NT, 128, 32 * 65]),
                        ("k_all", [NF, 128, L]), ("atok0", [128, D])]:
            dbg[nm] = nc.dram_tensor("dbg_" + nm, shp, BF16, kind="ExternalOutput").ap()

    te, ve, se, ge, sy = nc.tensor, nc.vector, nc.scalar, nc.gpsimd, nc.sync

    ctx = ExitStack()
    SB = lambda n, shp, dt: ctx.enter_context(nc.sbuf_tensor(n, shp, dt))
    sem = lambda n: ctx.enter_context(nc.semaphore(n))

    sW, sX, sT, sS, sV, sR = (sem(x) for x in ("sW", "sX", "sT", "sS", "sV", "sR"))
    sW1, sX1, sC = (sem(x) for x in ("sW1", "sX1", "sC"))
    cnt = {"W": 0, "X": 0, "T": 0, "S": 0, "V": 0, "R": 0, "W1": 0, "X1": 0, "C": 0}
    SEMS = {"W": sW, "X": sX, "T": sT, "S": sS, "V": sV, "R": sR,
            "W1": sW1, "X1": sX1, "C": sC}

    def inc(key, instr, n=None):
        n = (16 if key in ("W", "X", "W1", "X1", "C") else 1) if n is None else n
        cnt[key] += n
        instr.then_inc(SEMS[key], n)
        return cnt[key]

    def wait(eng, key, val=None):
        v = int(cnt[key] if val is None else val)
        if v > 0:
            eng.wait_ge(SEMS[key], v)

    ident = SB("ident", [128, 128], BF16)
    h128c = SB("h128c", [128, 128], BF16)
    p2c = SB("p2c", [128, 128], BF16)
    tric = SB("tric", [128, 128], BF16)
    su_ug = SB("su_ug", [128, NF], F32)
    wsu64su = SB("wsu64su", [128, NI], F32)
    wsg64 = SB("wsg64", [128, NI], F32)
    wsdp64 = SB("wsdp64", [128, NO], F32)
    h128su = SB("h128su", [128, NF * 128], BF16)
    tmp_pc = SB("tmp_pc", [128, NI], F32)
    su_dp_sb = SB("su_dp_sb", [128, NI], F32)
    wsu_sb = SB("wsu_sb", [128, NI], F32)

    rr = _sem_reset_range(nc)
    ge.dma_reset(rr)
    ge.sem_clear(rr)
    nc.all_engine_barrier()

    for src, dst in [(ident_in, ident), (h128_in, h128c), (p2_in, p2c), (tri_in, tric)]:
        inc("X", sy.dma_start(out=dst[:, :], in_=src[:, :]))
    inc("X", sy.dma_start(out=su_ug[:, :], in_=su_ug_pt[:, :]))
    inc("X", sy.dma_start(out=su_dp_sb[:, :], in_=su_dp_pt[:, :]))
    inc("X", sy.dma_start(out=wsu_sb[:, :], in_=wsu_pt[:, :]))
    inc("X", sy.dma_start(out=wsg64[:, :], in_=wsg_pt[:, :]))
    inc("X", sy.dma_start(out=wsdp64[:, :], in_=wsdp_pt[:, :]))
    wait(ve, "X")
    e0_ = inc("R", ve.tensor_scalar(tmp_pc[:, :], wsu_sb[:, :], 64.0, None, ALU.mult))
    ve.wait_ge(sR, e0_)
    ve.tensor_tensor(wsu64su[:, :], tmp_pc[:, :], su_dp_sb[:, :], ALU.mult)
    ve.tensor_scalar(wsg64[:, :], wsg64[:, :], 64.0, None, ALU.mult)
    ve.tensor_scalar(wsdp64[:, :], wsdp64[:, :], 64.0, None, ALU.mult)
    for fc in range(NF):
        ve.tensor_scalar(h128su[:, fc * 128:(fc + 1) * 128], h128c[:, :],
                         su_ug[:, fc:fc + 1], None, ALU.mult)
    nc.all_engine_barrier()

    with ExitStack() as scB:
        kT = [scB.enter_context(nc.sbuf_tensor(f"kT{kd}", [128, L], BF16))
              for kd in range(NF)]

        with ExitStack() as scB1:
            xnT = [scB1.enter_context(nc.sbuf_tensor(f"xnT{fc}", [128, L], BF16))
                   for fc in range(NF)]

            with ExitStack() as sc1:
                SB1 = lambda n, shp, dt: sc1.enter_context(nc.sbuf_tensor(n, shp, dt))
                PS1 = lambda n, shp, dt: sc1.enter_context(nc.psum_tensor(n, shp, dt))
                xbuf = [SB1(f"xbuf{i}", [128, D], F32) for i in range(2)]
                sq_scr = SB1("sq_scr", [128, D], F32)
                s_sum = SB1("s_sum", [128, 4], F32)
                s_mu = SB1("s_mu", [128, 4], F32)
                s_mu2 = SB1("s_mu2", [128, 4], F32)
                s_bias = SB1("s_bias", [128, 4], F32)
                s_sq = SB1("s_sq", [128, 4], F32)
                s_rstd = SB1("s_rstd", [128, 4], F32)
                s_std = SB1("s_std", [128, 4], F32)
                xn_tok = SB1("xn_tok", [128, D], BF16)
                ps_tr = [PS1(f"ps_tr{i}", [128, 128], BF16) for i in range(2)]

                apply_done = [0, 0]
                evac_done = [0, 0]
                for tt in range(NT):
                    xk = "X" if tt % 2 == 0 else "X1"
                    if tt >= 2:
                        wait(sy, "V", apply_done[tt % 2])
                    x_done = inc(xk, sy.dma_start(out=xbuf[tt % 2][:, :],
                                                  in_=xb[tt * 128:(tt + 1) * 128, :]))
                    wait(ve, xk, x_done)
                    e1 = inc("R", ve.tensor_reduce(s_sum[:, 0:1], xbuf[tt % 2][:, :],
                                                   AX.X, ALU.add))
                    ve.wait_ge(sR, e1)
                    e2 = inc("R", ve.tensor_scalar(s_mu[:, 0:1], s_sum[:, 0:1],
                                                   1.0 / D, None, ALU.mult))
                    ve.wait_ge(sR, e2)
                    e3 = inc("R", ve.tensor_tensor(s_mu2[:, 0:1], s_mu[:, 0:1],
                                                   s_mu[:, 0:1], ALU.mult))
                    ve.wait_ge(sR, e3)
                    vb = inc("V", ve.tensor_scalar(s_bias[:, 0:1], s_mu2[:, 0:1],
                                                   -1.0, 1e-5, ALU.mult, ALU.add))
                    wait(se, xk, x_done)
                    se.activation(sq_scr[:, :], xbuf[tt % 2][:, :], AF.Square,
                                  accum_out=s_sq[:, 0:1])
                    wait(se, "V", vb)
                    sr = inc("S", se.activation(s_std[:, 0:1], s_sq[:, 0:1], AF.Sqrt,
                                                bias=s_bias[:, 0:1], scale=1.0 / D))
                    wait(ve, "S", sr)
                    rr_ = inc("R", ve.reciprocal(s_rstd[:, 0:1], s_std[:, 0:1]))
                    ve.wait_ge(sR, rr_)
                    va = inc("V", ve.tensor_scalar(xn_tok[:, :], xbuf[tt % 2][:, :],
                                                   s_mu[:, 0:1], s_rstd[:, 0:1],
                                                   ALU.subtract, ALU.mult))
                    apply_done[tt % 2] = va
                    wait(te, "V", va)
                    for j in range(NF):
                        wait(te, "V", evac_done[j % 2])
                        tj = inc("T", te.transpose(ps_tr[j % 2][:, :],
                                                   xn_tok[:, j * 128:(j + 1) * 128],
                                                   ident[:, :]))
                        wait(ve, "T", tj)
                        evac_done[j % 2] = inc("V", ve.tensor_copy(
                            xnT[j][:, tt * 128:(tt + 1) * 128], ps_tr[j % 2][:, :]))
                for fc in range(NF):
                    wait(sy, "V")
                    inc("X", sy.dma_start(out=xn_scr[fc], in_=xnT[fc][:, :]))
                if debug:
                    inc("X", sy.dma_start(out=dbg["xnT0"][:, :], in_=xnT[0][:, :]))
                wait(sy, "X")
            nc.all_engine_barrier()

            with ExitStack() as sc2:
                SB2 = lambda n, shp, dt: sc2.enter_context(nc.sbuf_tensor(n, shp, dt))
                PS2 = lambda n, shp, dt: sc2.enter_context(nc.psum_tensor(n, shp, dt))
                ones_t = SB2("ones_t", [128, 32 * 65], BF16)
                wvb = [SB2(f"wvb{i}", [128, NF * 256], BF16) for i in range(2)]
                vstage = [SB2(f"vstage{i}", [128, 256], BF16) for i in range(2)]
                ps_v = [PS2(f"ps_v{i}", [128, 256], F32) for i in range(2)]
                on_ = inc("V", ve.memset(ones_t[:, :], 1.0))
                wait(sy, "V", on_)
                for tt in range(NT):
                    inc("X", sy.dma_start(out=v_scr[tt], in_=ones_t[:, :]))
                wait(sy, "X")
                wait(sy, "X1")
                NVC = 8
                wv_used = [0, 0]
                vst_dma = [0, 0]
                vst_evac = [0, 0]
                for ch in range(NVC):
                    vc, half = ch // 2, ch % 2
                    wk_ = "W" if ch % 2 == 0 else "W1"
                    if ch >= 2:
                        wait(ge, "T", wv_used[ch % 2])
                    wvd = inc(wk_, ge.dma_start(
                        out=wvb[ch % 2][:, :].rearrange("p (f j) -> p f j", f=NF),
                        in_=wv_blk[:, vc, :, half * 256:(half + 1) * 256]))
                    for tt in range(NT):
                        wait(te, wk_, wvd)
                        i2 = tt % 2
                        wait(te, "V", vst_evac[i2])
                        grp = None
                        for fc in range(NF):
                            grp = te.matmul(ps_v[i2][:, :],
                                            xnT[fc][:, tt * 128:(tt + 1) * 128],
                                            wvb[ch % 2][:, fc * 256:(fc + 1) * 256],
                                            start=(fc == 0), stop=(fc == NF - 1))
                        tg = inc("T", grp)
                        wv_used[ch % 2] = tg
                        wait(ve, "T", tg)
                        wait(ve, "X" if i2 == 0 else "X1", vst_dma[i2])
                        ev = inc("V", ve.tensor_copy(vstage[i2][:, :], ps_v[i2][:, :]))
                        vst_evac[i2] = ev
                        wait(sy, "V", ev)
                        xk2 = "X" if i2 == 0 else "X1"
                        vst_dma[i2] = inc(xk2, sy.dma_start(
                            out=v_scr[tt].rearrange("p (h c) -> p h c", c=65)[
                                :, ch * 4:(ch + 1) * 4, 0:64],
                            in_=vstage[i2][:, :].rearrange("p (h c) -> p h c", c=64)))
                wait(sy, "X")
                wait(sy, "X1")
            nc.all_engine_barrier()

            with ExitStack() as scQ:
                SBQ = lambda n, shp, dt: scQ.enter_context(nc.sbuf_tensor(n, shp, dt))
                PSQ = lambda n, shp, dt: scQ.enter_context(nc.psum_tensor(n, shp, dt))
                cosT = SBQ("cosT", [128, L], BF16)
                sinT = SBQ("sinT", [128, L], BF16)
                wkb = [SBQ(f"wkb{i}", [128, D], BF16) for i in range(2)]
                raw = [SBQ(f"raw{i}", [128, 512], BF16) for i in range(2)]
                rot = [SBQ(f"rot{i}", [128, 512], BF16) for i in range(2)]
                tmpa = SBQ("tmpa", [128, 512], BF16)
                tmpb = SBQ("tmpb", [128, 512], BF16)
                qstage = [SBQ(f"qstage{i}", [128, L], BF16) for i in range(2)]
                ps_k = [PSQ(f"ps_k{i}", [128, 512], F32) for i in range(3)]
                ps_r = [PSQ(f"ps_r{i}", [128, 512], F32) for i in range(2)]
                inc("C", sy.dma_start(out=cosT[:, :], in_=cos_in[:, :]))
                inc("C", sy.dma_start(out=sinT[:, :], in_=sin_in[:, :]))
                rope_ready = cnt["C"]

                def qk_phase(is_q):
                    entry_T = cnt["T"]
                    wk_used = [entry_T, entry_T]
                    q_spill = [0, 0]
                    rope_done = [0, 0, 0]
                    raw_rd = [0, 0]
                    for dt_i in range(NF):
                        blk_idx = dt_i if is_q else (16 + dt_i)
                        wait(ge, "T", wk_used[dt_i % 2])
                        wkk = "W" if dt_i % 2 == 0 else "W1"
                        wd = inc(wkk, ge.dma_start(
                            out=wkb[dt_i % 2][:, :].rearrange("p (f j) -> p f j", f=NF),
                            in_=wqkv_blk[:, blk_idx]))
                        dstT = qstage[dt_i % 2] if is_q else kT[dt_i]
                        vv = 0
                        for tc in range(4):
                            wait(te, wkk, wd)
                            wait(te, "V", rope_done[tc % 3])
                            grp = None
                            for fc in range(NF):
                                grp = te.matmul(ps_k[tc % 3][:, :],
                                                wkb[dt_i % 2][:, fc * 128:(fc + 1) * 128],
                                                xnT[fc][:, tc * 512:(tc + 1) * 512],
                                                start=(fc == 0), stop=(fc == NF - 1))
                            tg = inc("T", grp)
                            wk_used[dt_i % 2] = tg
                            wait(se, "T", tg)
                            if is_q:
                                wait(se, "X" if dt_i % 2 == 0 else "X1",
                                     q_spill[dt_i % 2])
                            if raw_rd[tc % 2]:
                                wait(se, "V", raw_rd[tc % 2])
                            sc = inc("S", se.activation(raw[tc % 2][:, :],
                                                        ps_k[tc % 3][:, :], AF.Copy))
                            wait(te, "S", sc)
                            tr = inc("T", te.matmul(ps_r[tc % 2][:, :], p2c[:, :],
                                                    raw[tc % 2][:, :],
                                                    start=True, stop=True))
                            wait(ve, "S", sc)
                            wait(ve, "C", rope_ready)
                            ve.tensor_tensor(tmpa[:, :], raw[tc % 2][:, :],
                                             cosT[:, tc * 512:(tc + 1) * 512], ALU.mult)
                            wait(ve, "T", tr)
                            ve.tensor_copy(rot[tc % 2][:, :], ps_r[tc % 2][:, :])
                            ve.tensor_tensor(tmpb[:, :], rot[tc % 2][:, :],
                                             sinT[:, tc * 512:(tc + 1) * 512], ALU.mult)
                            vv = inc("V", ve.tensor_tensor(
                                dstT[:, tc * 512:(tc + 1) * 512],
                                tmpa[:, :], tmpb[:, :], ALU.add))
                            rope_done[tc % 3] = vv
                            raw_rd[tc % 2] = vv
                        if is_q:
                            wait(sy, "V", vv)
                            q_spill[dt_i % 2] = inc(
                                "X" if dt_i % 2 == 0 else "X1",
                                sy.dma_start(out=q_scr[dt_i], in_=dstT[:, :]))
                    if is_q:
                        wait(sy, "X")
                        wait(sy, "X1")

                qk_phase(False)
                qk_phase(True)
                if debug:
                    wait(sy, "V")
                    inc("X", sy.dma_start(out=dbg["kT0"][:, :], in_=kT[0][:, :]))
                    for i_ in range(NF):
                        inc("X", sy.dma_start(out=dbg["k_all"][i_], in_=kT[i_][:, :]))
            nc.all_engine_barrier()

        with ExitStack() as sc3:
            SB3 = lambda n, shp, dt: sc3.enter_context(nc.sbuf_tensor(n, shp, dt))
            PS3 = lambda n, shp, dt: sc3.enter_context(nc.psum_tensor(n, shp, dt))
            v_aug = [SB3(f"vaug{tt}", [128, 32 * 65], BF16) for tt in range(NT)]
            qbuf = [SB3(f"qbuf{qd}", [128, 512], BF16) for qd in range(NF)]
            probs = [[SB3(f"probs{pp}_{kt}", [128, 512], BF16) for kt in range(NT)]
                     for pp in range(2)]
            attn_tok = [SB3(f"attn_tok{qs}", [128, D], BF16) for qs in range(4)]
            atb = [SB3(f"atb{i}", [128, 128], BF16) for i in range(2)]
            rec = SB3("rec", [128, 16], F32)
            ps_s = [PS3(f"ps_s{i}", [128, 512], F32) for i in range(3)]
            ps_av = [PS3(f"ps_av{i}", [128, 65], F32) for i in range(3)]
            ps_tr3 = [PS3(f"ps_tr3{i}", [128, 128], BF16) for i in range(2)]

            for tt in range(NT):
                inc("C", sy.dma_start(out=v_aug[tt][:, :], in_=v_scr[tt]))
            v_ready = cnt["C"]

            exp_done = np.zeros((2, NT), dtype=np.int64)
            exp_bank = [0, 0, 0]
            mask_done = np.zeros((2, NT), dtype=np.int64)
            av_last_user = np.zeros((2, NT), dtype=np.int64)
            norm_done = [0, 0, 0]
            atb_dma = [0, 0]
            first = True
            for qc in range(NQC):
                if qc >= 1:
                    wait(sy, "T")
                qld = [0] * NF
                for qd in range(NF):
                    qld[qd] = inc("C", sy.dma_start(
                        out=qbuf[qd][:, :], in_=q_scr[qd][:, qc * 512:(qc + 1) * 512]))
                qld = [qld[NF - 1]] * NF
                nkt = 4 * qc + 4
                hseq = (list(range(H)) if qc == 0 else []) + list(range(H))
                for h_i in hseq:
                    pp = h_i % 2
                    kd, off = h_i // 2, (h_i % 2) * 64
                    for kt in range(nkt):
                        jq0 = max(0, (kt - 4 * qc)) * 128
                        wait(te, "C", qld[kd])
                        if first:
                            wait(te, "C", v_ready)
                            first = False
                        wait(te, "S", exp_bank[kt % 3])
                        tg = inc("T", te.matmul(
                            ps_s[kt % 3][:, jq0:512],
                            kT[kd][off:off + 64, kt * 128:(kt + 1) * 128],
                            qbuf[kd][off:off + 64, jq0:512], start=True, stop=True))
                        wait(se, "T", tg)
                        wait(se, "T", av_last_user[pp][kt])
                        ex = inc("S", se.activation(probs[pp][kt][:, jq0:512],
                                                    ps_s[kt % 3][:, jq0:512],
                                                    AF.Exp, scale=0.125))
                        exp_done[pp][kt] = ex
                        exp_bank[kt % 3] = ex
                        if kt >= 4 * qc:
                            j = kt - 4 * qc
                            wait(ve, "S", ex)
                            mask_done[pp][kt] = inc("V", ve.tensor_tensor(
                                probs[pp][kt][:, j * 128:(j + 1) * 128],
                                probs[pp][kt][:, j * 128:(j + 1) * 128],
                                tric[:, :], ALU.mult))
                    rr_ = 0
                    for qs in range(4):
                        qs128 = qc * 4 + qs
                        wait(te, "S", exp_done[pp][qs128])
                        wait(te, "V", mask_done[pp][qs128])
                        wait(te, "V", norm_done[qs % 3])
                        grp = None
                        for kt in range(qs128 + 1):
                            grp = te.matmul(ps_av[qs % 3][:, :],
                                            probs[pp][kt][:, qs * 128:(qs + 1) * 128],
                                            v_aug[kt][:, h_i * 65:h_i * 65 + 65],
                                            start=(kt == 0), stop=(kt == qs128))
                        avg = inc("T", grp)
                        for kt in range(qs128 + 1):
                            av_last_user[pp][kt] = avg
                        wait(ve, "T", avg)
                        rr_ = inc("R", ve.reciprocal(rec[:, qs * 4:qs * 4 + 1],
                                                     ps_av[qs % 3][:, 64:65]))
                        if qs >= 1:
                            norm_done[(qs - 1) % 3] = inc("V", ve.tensor_scalar(
                                attn_tok[qs - 1][:, h_i * 64:(h_i + 1) * 64],
                                ps_av[(qs - 1) % 3][:, 0:64],
                                rec[:, (qs - 1) * 4:(qs - 1) * 4 + 1], None, ALU.mult))
                    ve.wait_ge(sR, rr_)
                    norm_done[3 % 3] = inc("V", ve.tensor_scalar(
                        attn_tok[3][:, h_i * 64:(h_i + 1) * 64],
                        ps_av[3 % 3][:, 0:64],
                        rec[:, 12:13], None, ALU.mult))
                if debug and qc == 0:
                    wait(sy, "V")
                    inc("X", sy.dma_start(out=dbg["atok0"][:, :], in_=attn_tok[0][:, :]))
                    wait(sy, "X")
                wait(te, "V")
                tr_evac = [0, 0]
                for qs in range(4):
                    for dt_i in range(NF):
                        i2 = dt_i % 2
                        wait(te, "V", tr_evac[i2])
                        tg = inc("T", te.transpose(
                            ps_tr3[i2][:, :],
                            attn_tok[qs][:, dt_i * 128:(dt_i + 1) * 128], ident[:, :]))
                        wait(ve, "T", tg)
                        wait(ve, "X" if i2 == 0 else "X1", atb_dma[i2])
                        tr_evac[i2] = inc("V", ve.tensor_copy(atb[i2][:, :],
                                                              ps_tr3[i2][:, :]))
                        wait(sy, "V", tr_evac[i2])
                        atb_dma[i2] = inc("X" if i2 == 0 else "X1", sy.dma_start(
                            out=at_scr[dt_i][:, qc * 512 + qs * 128:
                                             qc * 512 + (qs + 1) * 128],
                            in_=atb[i2][:, :]))
            wait(sy, "X")
            wait(sy, "X1")
            if debug:
                inc("X", sy.dma_start(out=dbg["att0"][:, :], in_=at_scr[0]))
                inc("X", sy.dma_start(out=dbg["q0"][:, :], in_=q_scr[0]))
                inc("X", sy.dma_start(out=dbg["v0"][:, :], in_=v_scr[0]))
                for i_ in range(NF):
                    inc("X", sy.dma_start(out=dbg["att_all"][i_], in_=at_scr[i_]))
                    inc("X", sy.dma_start(out=dbg["q_all"][i_], in_=q_scr[i_]))
                for i_ in range(NT):
                    inc("X", sy.dma_start(out=dbg["v_all"][i_], in_=v_scr[i_]))
        nc.all_engine_barrier()

    with ExitStack() as sc4:
        SB4 = lambda n, shp, dt: sc4.enter_context(nc.sbuf_tensor(n, shp, dt))
        PS4 = lambda n, shp, dt: sc4.enter_context(nc.psum_tensor(n, shp, dt))
        woT = SB4("woT", [128, NF * 2048], BF16)
        attb = [SB4(f"attb{dt_i}", [128, 512], BF16) for dt_i in range(NF)]
        s_stage = [SB4(f"s_stage{i}", [128, D], BF16) for i in range(2)]
        ps_wo = [PS4(f"ps_wo{i}", [128, 512], F32) for i in range(4)]
        wo_done = 0
        for g4 in range(4):
            wo_done = inc("W", ge.dma_start(
                out=woT[:, g4 * 4 * 2048:(g4 + 1) * 4 * 2048].rearrange(
                    "p (f j) -> p f j", f=4),
                in_=wo_blk[:, g4 * 4:(g4 + 1) * 4]))
        sp_done = [0, 0]
        ev_bank = [0, 0, 0, 0]
        for qc in range(NQC):
            if qc >= 1:
                wait(sy, "T")
            for dt_i in range(NF):
                inc("C", sy.dma_start(out=attb[dt_i][:, :],
                                      in_=at_scr[dt_i][:, qc * 512:(qc + 1) * 512]))
            att_ready = cnt["C"]
            for qs in range(4):
                ev = 0
                for oc in range(4):
                    wait(te, "W", wo_done)
                    wait(te, "C", att_ready)
                    wait(te, "S", ev_bank[oc])
                    grp = None
                    for fc in range(NF):
                        grp = te.matmul(
                            ps_wo[oc][:, :],
                            attb[fc][:, qs * 128:(qs + 1) * 128],
                            woT[:, fc * 2048 + oc * 512:fc * 2048 + (oc + 1) * 512],
                            start=(fc == 0), stop=(fc == NF - 1))
                    tg = inc("T", grp)
                    wait(se, "T", tg)
                    if oc == 0:
                        wait(se, "X" if qs % 2 == 0 else "X1", sp_done[qs % 2])
                    ev = inc("S", se.activation(
                        s_stage[qs % 2][:, oc * 512:(oc + 1) * 512],
                        ps_wo[oc][:, :], AF.Copy))
                    ev_bank[oc] = ev
                wait(sy, "S", ev)
                sp_done[qs % 2] = inc("X" if qs % 2 == 0 else "X1",
                                      sy.dma_start(out=s_scr[qc * 4 + qs],
                                                   in_=s_stage[qs % 2][:, :]))
        wait(sy, "X")
        wait(sy, "X1")
        if debug:
            inc("X", sy.dma_start(out=dbg["s0"][:, :], in_=s_scr[0]))
            for i_ in range(NT):
                inc("X", sy.dma_start(out=dbg["s_all"][i_], in_=s_scr[i_]))
    nc.all_engine_barrier()

    for th in range(NHALF):
        tb = th * TW
        with ExitStack() as sc5h:
            SBH = lambda n, shp, dt: sc5h.enter_context(nc.sbuf_tensor(n, shp, dt))
            hbuf = [SBH(f"h{th}_{it}", [128, TW], BF16) for it in range(NI)]

            with ExitStack() as sc6:
                SB6 = lambda n, shp, dt: sc6.enter_context(nc.sbuf_tensor(n, shp, dt))
                PS6 = lambda n, shp, dt: sc6.enter_context(nc.psum_tensor(n, shp, dt))
                ybuf = [SB6(f"y{th}_{fc}", [128, TW], BF16) for fc in range(NF)]
                xnb = [SB6(f"xnb{th}_{i}", [128, TW], BF16) for i in range(2)]
                wub = [SB6(f"wub{th}_{i}", [128, D], BF16) for i in range(2)]
                wgb = [SB6(f"wgb{th}_{i}", [128, D], BF16) for i in range(2)]
                sigb = [SB6(f"sigb{th}_{i}", [128, 512], F32) for i in range(2)]
                t1b = [SB6(f"t1b{th}_{i}", [128, 512], F32) for i in range(2)]
                ps_u = [PS6(f"ps_u{th}_{i}", [128, 512], F32) for i in range(4)]
                ps_g = [PS6(f"ps_g{th}_{i}", [128, 512], F32) for i in range(4)]

                xn_used = [0, 0]
                yev_bank = [0, 0, 0, 0]
                yev = 0
                for fc in range(NF):
                    xk = "X" if fc % 2 == 0 else "X1"
                    if fc >= 2:
                        wait(sy, "T", xn_used[fc % 2])
                    xld = inc(xk, sy.dma_start(out=xnb[fc % 2][:, :],
                                               in_=xn_scr[fc][:, tb:tb + TW]))
                    for t2 in range(2):
                        bk = (fc * 2 + t2) % 4
                        wait(te, xk, xld)
                        wait(te, "S", yev_bank[bk])
                        tg = inc("T", te.matmul(ps_u[bk][:, :],
                                                h128su[:, fc * 128:(fc + 1) * 128],
                                                xnb[fc % 2][:, t2 * 512:(t2 + 1) * 512],
                                                start=True, stop=True))
                        xn_used[fc % 2] = tg
                        wait(se, "T", tg)
                        yev = inc("S", se.activation(
                            ybuf[fc][:, t2 * 512:(t2 + 1) * 512],
                            ps_u[bk][:, :], AF.Copy, scale=C2048))
                        yev_bank[bk] = yev
                y_ready_t2 = [0, 0]
                wait(ve, "S", yev)
                for t2c in range(2):
                    c0, c1 = t2c * 512, (t2c + 1) * 512
                    hh = 1
                    while hh < NF:
                        for base in range(0, NF, 2 * hh):
                            for j in range(hh):
                                a, b = ybuf[base + j], ybuf[base + hh + j]
                                ve.tensor_tensor(a[:, c0:c1], a[:, c0:c1], b[:, c0:c1],
                                                 ALU.add)
                                ve.scalar_tensor_tensor(b[:, c0:c1], b[:, c0:c1], -2.0,
                                                        a[:, c0:c1], ALU.mult, ALU.add)
                        hh *= 2
                    y_ready_t2[t2c] = inc("V", ve.tensor_copy(tmp_pc[:, 0:1],
                                                              tmp_pc[:, 0:1]))
                if debug and th == 0:
                    wait(sy, "V", y_ready_t2[1])
                    inc("X", sy.dma_start(out=dbg["y0"][:, :], in_=ybuf[0][:, :]))

                wug_used = [0, 0]
                hdone_bank = [0, 0, 0, 0]
                sig_rd = [0, 0]
                hdone = 0
                for it in range(NI):
                    wk_ = "W" if it % 2 == 0 else "W1"
                    if it >= 2:
                        wait(ge, "T", wug_used[it % 2])
                    wu = inc(wk_, ge.dma_start(out=wub[it % 2][:, :].rearrange(
                        "p (f j) -> p f j", f=NF), in_=wu_blk[:, it]))
                    wg = inc(wk_, ge.dma_start(out=wgb[it % 2][:, :].rearrange(
                        "p (f j) -> p f j", f=NF), in_=wg_blk[:, it]))
                    for t2 in range(2):
                        bk = (it * 2 + t2) % 4
                        wait(te, wk_, wg)
                        wait(te, "V", y_ready_t2[t2])
                        wait(te, "V", hdone_bank[bk])
                        for fc in range(NF):
                            te.matmul(ps_u[bk][:, :],
                                      wub[it % 2][:, fc * 128:(fc + 1) * 128],
                                      ybuf[fc][:, t2 * 512:(t2 + 1) * 512],
                                      start=(fc == 0), stop=(fc == NF - 1))
                        gg = None
                        for fc in range(NF):
                            gg = te.matmul(ps_g[bk][:, :],
                                           wgb[it % 2][:, fc * 128:(fc + 1) * 128],
                                           ybuf[fc][:, t2 * 512:(t2 + 1) * 512],
                                           start=(fc == 0), stop=(fc == NF - 1))
                        tg = inc("T", gg)
                        wug_used[it % 2] = tg
                        wait(se, "T", tg)
                        wait(se, "V", sig_rd[t2])
                        sg_ = inc("S", se.activation(sigb[t2][:, :], ps_g[bk][:, :],
                                                     AF.Sigmoid,
                                                     scale=wsg64[:, it:it + 1]))
                        wait(ve, "S", sg_)
                        ve.scalar_tensor_tensor(t1b[t2][:, :], ps_g[bk][:, :],
                                                wsg64[:, it:it + 1], sigb[t2][:, :],
                                                ALU.mult, ALU.mult)
                        hdone = inc("V", ve.scalar_tensor_tensor(
                            hbuf[it][:, t2 * 512:(t2 + 1) * 512], ps_u[bk][:, :],
                            wsu64su[:, it:it + 1], t1b[t2][:, :], ALU.mult, ALU.mult))
                        hdone_bank[bk] = hdone
                        sig_rd[t2] = hdone

                hev_bank = [0, 0, 0, 0]
                hev_t2 = [0, 0]
                h_ready_t2 = [0, 0]
                hev = 0
                wait(te, "V", hdone)
                for t2 in range(2):
                    for it in range(NI):
                        bk = (t2 * NI + it) % 4
                        wait(te, "S", hev_bank[bk])
                        if t2 == 1 and it < 4:
                            pass
                        tg = inc("T", te.matmul(ps_u[bk][:, :], h128c[:, :],
                                                hbuf[it][:, t2 * 512:(t2 + 1) * 512],
                                                start=True, stop=True))
                        wait(se, "T", tg)
                        hev = inc("S", se.activation(
                            hbuf[it][:, t2 * 512:(t2 + 1) * 512],
                            ps_u[bk][:, :], AF.Copy, scale=C8192))
                        hev_bank[bk] = hev
                    hev_t2[t2] = hev
                    c0, c1 = t2 * 512, (t2 + 1) * 512
                    wait(ve, "S", hev_t2[t2])
                    hh = 1
                    while hh < NI:
                        for base in range(0, NI, 2 * hh):
                            for j in range(hh):
                                a, b = hbuf[base + j], hbuf[base + hh + j]
                                ve.tensor_tensor(a[:, c0:c1], a[:, c0:c1], b[:, c0:c1],
                                                 ALU.add)
                                ve.scalar_tensor_tensor(b[:, c0:c1], b[:, c0:c1], -2.0,
                                                        a[:, c0:c1], ALU.mult, ALU.add)
                        hh *= 2
                    h_ready_t2[t2] = inc("V", ve.tensor_copy(tmp_pc[:, 0:1],
                                                             tmp_pc[:, 0:1]))
                if debug and th == 0:
                    wait(sy, "V", h_ready_t2[1])
                    inc("X", sy.dma_start(out=dbg["h0"][:, :], in_=hbuf[0][:, :]))

            with ExitStack() as sc7:
                SB7 = lambda n, shp, dt: sc7.enter_context(nc.sbuf_tensor(n, shp, dt))
                PS7 = lambda n, shp, dt: sc7.enter_context(nc.psum_tensor(n, shp, dt))
                wdb = [SB7(f"wdb{th}_{i}", [128, 32 * 128], BF16) for i in range(2)]
                gstage = [SB7(f"gstage{th}_{i}", [128, TW], BF16) for i in range(2)]
                sigd = [SB7(f"sigd{th}_{i}", [128, 512], F32) for i in range(4)]
                ps_d = [PS7(f"ps_d{th}_{i}", [128, 512], F32) for i in range(8)]
                wait(ge, "T")
                gdone_bank = [0] * 8
                wd_used = [0, 0]
                sigd_rd = [0, 0, 0, 0]
                for t2 in range(2):
                    g_spill2 = [0, 0]
                    for ot in range(NO):
                        bk = (t2 * 16 + ot) % 8
                        wd_done = [0, 0]
                        for hf in range(2):
                            if t2 + ot > 0:
                                wait(ge, "T", wd_used[hf])
                            wd_done[hf] = inc("W" if hf == 0 else "W1", ge.dma_start(
                                out=wdb[hf][:, :].rearrange("p (i j) -> p i j", i=32),
                                in_=wd_blk[:, ot, hf * 32:(hf + 1) * 32]))
                        wait(te, "V", h_ready_t2[t2])
                        wait(te, "V", gdone_bank[bk])
                        grp = None
                        for it in range(NI):
                            hf, il = it // 32, it % 32
                            if il == 0:
                                wait(te, "W" if hf == 0 else "W1", wd_done[hf])
                            grp = te.matmul(ps_d[bk][:, :],
                                            wdb[hf][:, il * 128:(il + 1) * 128],
                                            hbuf[it][:, t2 * 512:(t2 + 1) * 512],
                                            start=(it == 0), stop=(it == NI - 1))
                        tg = inc("T", grp)
                        wd_used[0] = wd_used[1] = tg
                        wait(se, "T", tg)
                        wait(se, "V", sigd_rd[ot % 4])
                        sd = inc("S", se.activation(sigd[ot % 4][:, :], ps_d[bk][:, :],
                                                    AF.Sigmoid,
                                                    scale=wsdp64[:, ot:ot + 1]))
                        wait(ve, "S", sd)
                        wait(ve, "X" if ot % 2 == 0 else "X1", g_spill2[ot % 2])
                        gdone = inc("V", ve.scalar_tensor_tensor(
                            gstage[ot % 2][:, t2 * 512:(t2 + 1) * 512], ps_d[bk][:, :],
                            wsdp64[:, ot:ot + 1], sigd[ot % 4][:, :],
                            ALU.mult, ALU.mult))
                        gdone_bank[bk] = gdone
                        sigd_rd[ot % 4] = gdone
                        wait(sy, "V", gdone)
                        g_spill2[ot % 2] = inc("X" if ot % 2 == 0 else "X1",
                                               sy.dma_start(
                            out=g_scr[ot][:, t2 * 512:(t2 + 1) * 512],
                            in_=gstage[ot % 2][:, t2 * 512:(t2 + 1) * 512]))
                if debug and th == 0:
                    wait(sy, "X")
                    inc("X", sy.dma_start(out=dbg["g0"][:, :], in_=g_scr[0]))
                wait(sy, "X")
                wait(sy, "X1")

        with ExitStack() as sc8:
            SB8 = lambda n, shp, dt: sc8.enter_context(nc.sbuf_tensor(n, shp, dt))
            PS8 = lambda n, shp, dt: sc8.enter_context(nc.psum_tensor(n, shp, dt))
            gcol = [SB8(f"gcol{th}_{i}", [128, NO * 128], BF16) for i in range(2)]
            gtok = [SB8(f"gtok{th}_{i}", [128, D], BF16) for i in range(2)]
            xbuf2 = [SB8(f"xbuf2{th}_{i}", [128, D], F32) for i in range(2)]
            sbuf2 = [SB8(f"sbuf2{th}_{i}", [128, D], BF16) for i in range(2)]
            sgb = SB8(f"sgb{th}", [128, D], F32)
            obuf = [SB8(f"obuf{th}_{i}", [128, D], F32) for i in range(2)]
            ps_t8 = [PS8(f"ps_t8{th}_{i}", [128, 128], BF16) for i in range(4)]
            ost = [0, 0]
            comb = [0, 0]
            gused = [0, 0]
            gev_bank = [0, 0, 0, 0]
            for ts in range(8):
                tt = th * 8 + ts
                xk = "X" if ts % 2 == 0 else "X1"
                wait(sy, "V", comb[ts % 2])
                wait(sy, "T", gused[ts % 2])
                gl = 0
                for ot in range(NO):
                    gl = inc(xk, sy.dma_start(
                        out=gcol[ts % 2][:, ot * 128:(ot + 1) * 128],
                        in_=g_scr[ot][:, ts * 128:(ts + 1) * 128]))
                xl = inc(xk, sy.dma_start(out=xbuf2[ts % 2][:, :],
                                          in_=xb[tt * 128:(tt + 1) * 128, :]))
                sl = inc(xk, sy.dma_start(out=sbuf2[ts % 2][:, :], in_=s_scr[tt]))
                gev = 0
                for ot in range(NO):
                    wait(te, xk, sl)
                    wait(te, "V", gev_bank[ot % 4])
                    tg = inc("T", te.transpose(ps_t8[ot % 4][:, :],
                                               gcol[ts % 2][:, ot * 128:(ot + 1) * 128],
                                               ident[:, :]))
                    gused[ts % 2] = tg
                    wait(ve, "T", tg)
                    if ot == 0:
                        wait(ve, xk, ost[ts % 2])
                    gev = inc("V", ve.tensor_copy(
                        gtok[ts % 2][:, ot * 128:(ot + 1) * 128], ps_t8[ot % 4][:, :]))
                    gev_bank[ot % 4] = gev
                wait(ve, xk, sl)
                ve.tensor_tensor(sgb[:, :], sbuf2[ts % 2][:, :], gtok[ts % 2][:, :],
                                 ALU.mult)
                comb[ts % 2] = inc("V", ve.tensor_tensor(
                    obuf[ts % 2][:, :], xbuf2[ts % 2][:, :], sgb[:, :], ALU.add))
                wait(sy, "V", comb[ts % 2])
                ost[ts % 2] = inc(xk, sy.dma_start(
                    out=out[tt * 128:(tt + 1) * 128, :], in_=obuf[ts % 2][:, :]))
            wait(sy, "X")
            wait(sy, "X1")
        nc.all_engine_barrier()

    ge.sem_clear(rr)
    nc.all_engine_barrier()
    ctx.close()
    return nc


def host_prep(inputs):
    wqkv = np.asarray(inputs["L0_attn_wqkv"], np.float32)
    wo = np.asarray(inputs["L0_attn_wo"], np.float32)
    wu = np.asarray(inputs["L0_ffn_W_up"], np.float32)
    wg = np.asarray(inputs["L0_ffn_W_gate"], np.float32)
    wd = np.asarray(inputs["L0_ffn_W_down"], np.float32)
    su_ug = np.asarray(inputs["L0_ffn_SU_ug"], np.float32)
    su_dp = np.asarray(inputs["L0_ffn_SU_dp"], np.float32)
    wsc_ug = np.asarray(inputs["L0_ffn_Wscale_ug"], np.float32)
    wsc_dp = np.asarray(inputs["L0_ffn_Wscale_dp"], np.float32)

    qk = wqkv[0:4096]
    wqkv_blk = np.ascontiguousarray(qk.reshape(32, 128, 16, 128).transpose(3, 0, 2, 1))
    wv = wqkv[4096:6144]
    wv_blk = np.ascontiguousarray(wv.reshape(4, 512, 16, 128).transpose(3, 0, 2, 1))
    wo_blk = np.ascontiguousarray(wo.T.reshape(16, 128, 2048).transpose(1, 0, 2))
    wu_blk = np.ascontiguousarray(wu.reshape(64, 128, 16, 128).transpose(3, 0, 2, 1))
    wg_blk = np.ascontiguousarray(wg.reshape(64, 128, 16, 128).transpose(3, 0, 2, 1))
    wd_blk = np.ascontiguousarray(wd.reshape(16, 128, 64, 128).transpose(3, 0, 2, 1))
    consts = build_host_consts()
    base = dict(
        wqkv_blk=wqkv_blk, wv_blk=wv_blk, wo_blk=wo_blk, wu_blk=wu_blk,
        wg_blk=wg_blk, wd_blk=wd_blk,
        su_ug_pt=np.ascontiguousarray(su_ug.reshape(16, 128).T),
        su_dp_pt=np.ascontiguousarray(su_dp.reshape(64, 128).T),
        wsu_pt=np.ascontiguousarray(wsc_ug[:INTER].reshape(64, 128).T),
        wsg_pt=np.ascontiguousarray(wsc_ug[INTER:].reshape(64, 128).T),
        wsdp_pt=np.ascontiguousarray(wsc_dp.reshape(16, 128).T),
        ident_in=consts["ident"], h128_in=consts["h128"], p2_in=consts["p2"],
        tri_in=consts["tri"], cos_in=consts["cosT"], sin_in=consts["sinT"])
    x = np.asarray(inputs["L0_attn_x"], np.float32)
    maps = []
    for b in range(8):
        m = dict(base)
        m["xb"] = np.ascontiguousarray(x[b])
        maps.append(m)
    return maps


def kernel(**inputs):
    maps = host_prep(inputs)
    nc = build_program(debug=False)
    res = run_bass_kernel_spmd(nc, maps, core_ids=list(range(8)))
    return np.stack([np.asarray(r["out"], np.float32) for r in res.results], axis=0)




# version 08 (final)
# speedup vs baseline: 1.2204x; 1.0105x over previous
import numpy as np
import ml_dtypes
from contextlib import ExitStack
import concourse.bass as bass
import concourse.mybir as mybir
from concourse.bass_utils import run_bass_kernel_spmd

F32 = mybir.dt.float32
BF16 = mybir.dt.bfloat16
AF = mybir.ActivationFunctionType
ALU = mybir.AluOpType
AX = mybir.AxisListType

L = 2048
D = 2048
NF = 16
H = 32
HD = 64
INTER = 8192
NI = 64
NO = 16
NT = 16
NQC = 4
NHALF = 2
TW = 1024
C2048 = float(1.0 / (64.0 * np.sqrt(2048.0)))
C8192 = float(1.0 / (64.0 * np.sqrt(8192.0)))
BF = ml_dtypes.bfloat16


def _sem_reset_range(nc):
    ksr = nc._kernel_sem_range
    start = ksr.start
    mono = start + (3 if nc._bir_kernel_barrier_sem is not None else 2)
    return range(mono + len(nc._monotonic_sems), ksr.stop)


def _hadamard(n):
    h = np.array([[1.0]], dtype=np.float32)
    while h.shape[0] < n:
        h = np.block([[h, h], [h, -h]])
    return h


def build_host_consts():
    ident = np.eye(128, dtype=BF)
    h128 = _hadamard(128).astype(BF)
    p2 = np.zeros((128, 128), dtype=np.float32)
    for d in range(128):
        e = (d % 64 + 32) % 64 + 64 * (d // 64)
        p2[e, d] = 1.0
    p2 = p2.astype(BF)
    ki, qi = np.meshgrid(np.arange(128), np.arange(128), indexing="ij")
    tri = (ki <= qi).astype(np.float32).astype(BF)
    inv_freq = 1.0 / 512.0 ** (np.arange(0, HD, 2, dtype=np.float32) / HD)
    t = np.arange(L, dtype=np.float32)
    didx = np.arange(128) % 64
    fr = t[None, :] * inv_freq[didx % 32][:, None]
    cosT = np.cos(fr).astype(BF)
    sgn = np.where(didx < 32, -1.0, 1.0).astype(np.float32)
    sinT = (np.sin(fr) * sgn[:, None]).astype(BF)
    return dict(ident=ident, h128=h128, p2=p2, tri=tri,
                cosT=np.ascontiguousarray(cosT), sinT=np.ascontiguousarray(sinT))


SER = 0

def build_program(debug=False):
    nc = bass.Bass("TRN2", target_bir_lowering=False)

    xb = nc.dram_tensor("xb", [L, D], F32, kind="ExternalInput").ap()
    wqkv_blk = nc.dram_tensor("wqkv_blk", [128, 32, NF, 128], F32, kind="ExternalInput").ap()
    wv_blk = nc.dram_tensor("wv_blk", [128, 4, NF, 512], F32, kind="ExternalInput").ap()
    wo_blk = nc.dram_tensor("wo_blk", [128, NF, D], F32, kind="ExternalInput").ap()
    wu_blk = nc.dram_tensor("wu_blk", [128, NI, NF, 128], F32, kind="ExternalInput").ap()
    wg_blk = nc.dram_tensor("wg_blk", [128, NI, NF, 128], F32, kind="ExternalInput").ap()
    wd_blk = nc.dram_tensor("wd_blk", [128, NO, NI, 128], F32, kind="ExternalInput").ap()
    su_ug_pt = nc.dram_tensor("su_ug_pt", [128, NF], F32, kind="ExternalInput").ap()
    su_dp_pt = nc.dram_tensor("su_dp_pt", [128, NI], F32, kind="ExternalInput").ap()
    wsu_pt = nc.dram_tensor("wsu_pt", [128, NI], F32, kind="ExternalInput").ap()
    wsg_pt = nc.dram_tensor("wsg_pt", [128, NI], F32, kind="ExternalInput").ap()
    wsdp_pt = nc.dram_tensor("wsdp_pt", [128, NO], F32, kind="ExternalInput").ap()
    ident_in = nc.dram_tensor("ident_in", [128, 128], BF16, kind="ExternalInput").ap()
    h128_in = nc.dram_tensor("h128_in", [128, 128], BF16, kind="ExternalInput").ap()
    p2_in = nc.dram_tensor("p2_in", [128, 128], BF16, kind="ExternalInput").ap()
    tri_in = nc.dram_tensor("tri_in", [128, 128], BF16, kind="ExternalInput").ap()
    cos_in = nc.dram_tensor("cos_in", [128, L], BF16, kind="ExternalInput").ap()
    sin_in = nc.dram_tensor("sin_in", [128, L], BF16, kind="ExternalInput").ap()
    out = nc.dram_tensor("out", [L, D], F32, kind="ExternalOutput").ap()

    xn_scr = nc.dram_tensor("xn_scr", [NF, 128, L], BF16, kind="Internal").ap()
    q_scr = nc.dram_tensor("q_scr", [NF, 128, L], BF16, kind="Internal").ap()
    v_scr = nc.dram_tensor("v_scr", [NT, 128, 32 * 65], BF16, kind="Internal").ap()
    at_scr = nc.dram_tensor("at_scr", [NF, 128, L], BF16, kind="Internal").ap()
    s_scr = nc.dram_tensor("s_scr", [NT, 128, D], BF16, kind="Internal").ap()
    g_scr = nc.dram_tensor("g_scr", [NO, 128, TW], BF16, kind="Internal").ap()

    dbg = {}
    if debug:
        for nm, shp in [("xnT0", [128, L]), ("kT0", [128, L]), ("q0", [128, L]),
                        ("v0", [128, 32 * 65]), ("att0", [128, L]), ("s0", [128, D]),
                        ("y0", [128, TW]), ("h0", [128, TW]), ("g0", [128, TW]),
                        ("att_all", [NF, 128, L]), ("s_all", [NT, 128, D]),
                        ("q_all", [NF, 128, L]), ("v_all", [NT, 128, 32 * 65]),
                        ("k_all", [NF, 128, L]), ("atok0", [128, D])]:
            dbg[nm] = nc.dram_tensor("dbg_" + nm, shp, BF16, kind="ExternalOutput").ap()

    te, ve, se, ge, sy = nc.tensor, nc.vector, nc.scalar, nc.gpsimd, nc.sync

    ctx = ExitStack()
    SB = lambda n, shp, dt: ctx.enter_context(nc.sbuf_tensor(n, shp, dt))
    sem = lambda n: ctx.enter_context(nc.semaphore(n))

    sW, sX, sT, sS, sV, sR = (sem(x) for x in ("sW", "sX", "sT", "sS", "sV", "sR"))
    sW1, sX1, sC = (sem(x) for x in ("sW1", "sX1", "sC"))
    cnt = {"W": 0, "X": 0, "T": 0, "S": 0, "V": 0, "R": 0, "W1": 0, "X1": 0, "C": 0}
    SEMS = {"W": sW, "X": sX, "T": sT, "S": sS, "V": sV, "R": sR,
            "W1": sW1, "X1": sX1, "C": sC}

    def inc(key, instr, n=None):
        n = (16 if key in ("W", "X", "W1", "X1", "C") else 1) if n is None else n
        cnt[key] += n
        instr.then_inc(SEMS[key], n)
        return cnt[key]

    def wait(eng, key, val=None):
        v = int(cnt[key] if val is None else val)
        if v > 0:
            eng.wait_ge(SEMS[key], v)

    ident = SB("ident", [128, 128], BF16)
    h128c = SB("h128c", [128, 128], BF16)
    p2c = SB("p2c", [128, 128], BF16)
    tric = SB("tric", [128, 128], BF16)
    su_ug = SB("su_ug", [128, NF], F32)
    wsu64su = SB("wsu64su", [128, NI], F32)
    wsg64 = SB("wsg64", [128, NI], F32)
    wsdp64 = SB("wsdp64", [128, NO], F32)
    h128su = SB("h128su", [128, NF * 128], BF16)
    tmp_pc = SB("tmp_pc", [128, NI], F32)
    su_dp_sb = SB("su_dp_sb", [128, NI], F32)
    wsu_sb = SB("wsu_sb", [128, NI], F32)

    rr = _sem_reset_range(nc)
    ge.dma_reset(rr)
    ge.sem_clear(rr)
    nc.all_engine_barrier()

    for src, dst in [(ident_in, ident), (h128_in, h128c), (p2_in, p2c), (tri_in, tric)]:
        inc("X", sy.dma_start(out=dst[:, :], in_=src[:, :]))
    inc("X", sy.dma_start(out=su_ug[:, :], in_=su_ug_pt[:, :]))
    inc("X", sy.dma_start(out=su_dp_sb[:, :], in_=su_dp_pt[:, :]))
    inc("X", sy.dma_start(out=wsu_sb[:, :], in_=wsu_pt[:, :]))
    inc("X", sy.dma_start(out=wsg64[:, :], in_=wsg_pt[:, :]))
    inc("X", sy.dma_start(out=wsdp64[:, :], in_=wsdp_pt[:, :]))
    wait(ve, "X")
    e0_ = inc("R", ve.tensor_scalar(tmp_pc[:, :], wsu_sb[:, :], 64.0, None, ALU.mult))
    ve.wait_ge(sR, e0_)
    ve.tensor_tensor(wsu64su[:, :], tmp_pc[:, :], su_dp_sb[:, :], ALU.mult)
    ve.tensor_scalar(wsg64[:, :], wsg64[:, :], 64.0, None, ALU.mult)
    ve.tensor_scalar(wsdp64[:, :], wsdp64[:, :], 64.0, None, ALU.mult)
    for fc in range(NF):
        ve.tensor_scalar(h128su[:, fc * 128:(fc + 1) * 128], h128c[:, :],
                         su_ug[:, fc:fc + 1], None, ALU.mult)
    nc.all_engine_barrier()

    with ExitStack() as scB:
        kT = [scB.enter_context(nc.sbuf_tensor(f"kT{kd}", [128, L], BF16))
              for kd in range(NF)]

        with ExitStack() as scB1:
            xnT = [scB1.enter_context(nc.sbuf_tensor(f"xnT{fc}", [128, L], BF16))
                   for fc in range(NF)]

            with ExitStack() as sc1:
                SB1 = lambda n, shp, dt: sc1.enter_context(nc.sbuf_tensor(n, shp, dt))
                PS1 = lambda n, shp, dt: sc1.enter_context(nc.psum_tensor(n, shp, dt))
                xbuf = [SB1(f"xbuf{i}", [128, D], F32) for i in range(2)]
                sq_scr = SB1("sq_scr", [128, D], F32)
                s_sum = SB1("s_sum", [128, 4], F32)
                s_mu = SB1("s_mu", [128, 4], F32)
                s_mu2 = SB1("s_mu2", [128, 4], F32)
                s_bias = SB1("s_bias", [128, 4], F32)
                s_sq = SB1("s_sq", [128, 4], F32)
                s_rstd = SB1("s_rstd", [128, 4], F32)
                s_std = SB1("s_std", [128, 4], F32)
                xn_tok = SB1("xn_tok", [128, D], BF16)
                ps_tr = [PS1(f"ps_tr{i}", [128, 128], BF16) for i in range(2)]

                apply_done = [0, 0]
                evac_done = [0, 0]
                sr_prev = 0
                for tt in range(NT):
                    xk = "X" if tt % 2 == 0 else "X1"
                    if tt >= 2:
                        wait(sy, "V", apply_done[tt % 2])
                    x_done = inc(xk, sy.dma_start(out=xbuf[tt % 2][:, :],
                                                  in_=xb[tt * 128:(tt + 1) * 128, :]))
                    wait(ve, xk, x_done)
                    e1 = inc("R", ve.tensor_reduce(s_sum[:, 0:1], xbuf[tt % 2][:, :],
                                                   AX.X, ALU.add))
                    ve.wait_ge(sR, e1)
                    e2 = inc("R", ve.tensor_scalar(s_mu[:, 0:1], s_sum[:, 0:1],
                                                   1.0 / D, None, ALU.mult))
                    ve.wait_ge(sR, e2)
                    e3 = inc("R", ve.tensor_tensor(s_mu2[:, 0:1], s_mu[:, 0:1],
                                                   s_mu[:, 0:1], ALU.mult))
                    ve.wait_ge(sR, e3)
                    vb = inc("V", ve.tensor_scalar(s_bias[:, 0:1], s_mu2[:, 0:1],
                                                   -1.0, 1e-5, ALU.mult, ALU.add))
                    wait(se, xk, x_done)
                    se.wait_ge(sS, sr_prev) if sr_prev else None
                    sqd = inc("S", se.activation(sq_scr[:, :], xbuf[tt % 2][:, :],
                                                 AF.Square, accum_out=s_sq[:, 0:1]))
                    se.wait_ge(sS, sqd)
                    wait(se, "V", vb)
                    sr = inc("S", se.activation(s_std[:, 0:1], s_sq[:, 0:1], AF.Sqrt,
                                                bias=s_bias[:, 0:1], scale=1.0 / D))
                    sr_prev = sr
                    wait(ve, "S", sr)
                    rr_ = inc("R", ve.reciprocal(s_rstd[:, 0:1], s_std[:, 0:1]))
                    ve.wait_ge(sR, rr_)
                    va = inc("V", ve.tensor_scalar(xn_tok[:, :], xbuf[tt % 2][:, :],
                                                   s_mu[:, 0:1], s_rstd[:, 0:1],
                                                   ALU.subtract, ALU.mult))
                    apply_done[tt % 2] = va
                    wait(te, "V", va)
                    for j in range(NF):
                        wait(te, "V", evac_done[j % 2])
                        tj = inc("T", te.transpose(ps_tr[j % 2][:, :],
                                                   xn_tok[:, j * 128:(j + 1) * 128],
                                                   ident[:, :]))
                        wait(ve, "T", tj)
                        evac_done[j % 2] = inc("V", ve.tensor_copy(
                            xnT[j][:, tt * 128:(tt + 1) * 128], ps_tr[j % 2][:, :]))
                for fc in range(NF):
                    wait(sy, "V")
                    inc("X", sy.dma_start(out=xn_scr[fc], in_=xnT[fc][:, :]))
                if debug:
                    inc("X", sy.dma_start(out=dbg["xnT0"][:, :], in_=xnT[0][:, :]))
                wait(sy, "X")
            nc.all_engine_barrier()

            with ExitStack() as sc2:
                SB2 = lambda n, shp, dt: sc2.enter_context(nc.sbuf_tensor(n, shp, dt))
                PS2 = lambda n, shp, dt: sc2.enter_context(nc.psum_tensor(n, shp, dt))
                ones_t = SB2("ones_t", [128, 32 * 65], BF16)
                wvb = [SB2(f"wvb{i}", [128, NF * 256], BF16) for i in range(2)]
                vstage = [SB2(f"vstage{i}", [128, 256], BF16) for i in range(2)]
                ps_v = [PS2(f"ps_v{i}", [128, 256], F32) for i in range(2)]
                on_ = inc("V", ve.memset(ones_t[:, :], 1.0))
                wait(sy, "V", on_)
                for tt in range(NT):
                    inc("X", sy.dma_start(out=v_scr[tt], in_=ones_t[:, :]))
                wait(sy, "X")
                wait(sy, "X1")
                NVC = 8
                wv_used = [0, 0]
                vst_dma = [0, 0]
                vst_evac = [0, 0]
                for ch in range(NVC):
                    vc, half = ch // 2, ch % 2
                    wk_ = "W" if ch % 2 == 0 else "W1"
                    if ch >= 2:
                        wait(ge, "T", wv_used[ch % 2])
                    wvd = inc(wk_, ge.dma_start(
                        out=wvb[ch % 2][:, :].rearrange("p (f j) -> p f j", f=NF),
                        in_=wv_blk[:, vc, :, half * 256:(half + 1) * 256]))
                    for tt in range(NT):
                        wait(te, wk_, wvd)
                        i2 = tt % 2
                        wait(te, "V", vst_evac[i2])
                        grp = None
                        for fc in range(NF):
                            grp = te.matmul(ps_v[i2][:, :],
                                            xnT[fc][:, tt * 128:(tt + 1) * 128],
                                            wvb[ch % 2][:, fc * 256:(fc + 1) * 256],
                                            start=(fc == 0), stop=(fc == NF - 1))
                        tg = inc("T", grp)
                        wv_used[ch % 2] = tg
                        wait(ve, "T", tg)
                        wait(ve, "X" if i2 == 0 else "X1", vst_dma[i2])
                        ev = inc("V", ve.tensor_copy(vstage[i2][:, :], ps_v[i2][:, :]))
                        vst_evac[i2] = ev
                        wait(sy, "V", ev)
                        xk2 = "X" if i2 == 0 else "X1"
                        vst_dma[i2] = inc(xk2, sy.dma_start(
                            out=v_scr[tt].rearrange("p (h c) -> p h c", c=65)[
                                :, ch * 4:(ch + 1) * 4, 0:64],
                            in_=vstage[i2][:, :].rearrange("p (h c) -> p h c", c=64)))
                wait(sy, "X")
                wait(sy, "X1")
            nc.all_engine_barrier()

            with ExitStack() as scQ:
                SBQ = lambda n, shp, dt: scQ.enter_context(nc.sbuf_tensor(n, shp, dt))
                PSQ = lambda n, shp, dt: scQ.enter_context(nc.psum_tensor(n, shp, dt))
                cosT = SBQ("cosT", [128, L], BF16)
                sinT = SBQ("sinT", [128, L], BF16)
                wkb = [SBQ(f"wkb{i}", [128, D], BF16) for i in range(2)]
                raw = [SBQ(f"raw{i}", [128, 512], BF16) for i in range(2)]
                rot = [SBQ(f"rot{i}", [128, 512], BF16) for i in range(2)]
                tmpa = SBQ("tmpa", [128, 512], BF16)
                tmpb = SBQ("tmpb", [128, 512], BF16)
                qstage = [SBQ(f"qstage{i}", [128, L], BF16) for i in range(2)]
                ps_k = [PSQ(f"ps_k{i}", [128, 512], F32) for i in range(3)]
                ps_r = [PSQ(f"ps_r{i}", [128, 512], F32) for i in range(2)]
                inc("C", sy.dma_start(out=cosT[:, :], in_=cos_in[:, :]))
                inc("C", sy.dma_start(out=sinT[:, :], in_=sin_in[:, :]))
                rope_ready = cnt["C"]

                def qk_phase(is_q):
                    entry_T = cnt["T"]
                    wk_used = [entry_T, entry_T]
                    q_spill = [0, 0]
                    rope_done = [0, 0, 0]
                    raw_rd = [0, 0]
                    for dt_i in range(NF):
                        blk_idx = dt_i if is_q else (16 + dt_i)
                        wait(ge, "T", wk_used[dt_i % 2])
                        wkk = "W" if dt_i % 2 == 0 else "W1"
                        wd = inc(wkk, ge.dma_start(
                            out=wkb[dt_i % 2][:, :].rearrange("p (f j) -> p f j", f=NF),
                            in_=wqkv_blk[:, blk_idx]))
                        dstT = qstage[dt_i % 2] if is_q else kT[dt_i]
                        vv = 0
                        for tc in range(4):
                            wait(te, wkk, wd)
                            wait(te, "V", rope_done[tc % 3])
                            grp = None
                            for fc in range(NF):
                                grp = te.matmul(ps_k[tc % 3][:, :],
                                                wkb[dt_i % 2][:, fc * 128:(fc + 1) * 128],
                                                xnT[fc][:, tc * 512:(tc + 1) * 512],
                                                start=(fc == 0), stop=(fc == NF - 1))
                            tg = inc("T", grp)
                            wk_used[dt_i % 2] = tg
                            wait(se, "T", tg)
                            if is_q:
                                wait(se, "X" if dt_i % 2 == 0 else "X1",
                                     q_spill[dt_i % 2])
                            if raw_rd[tc % 2]:
                                wait(se, "V", raw_rd[tc % 2])
                            sc = inc("S", se.activation(raw[tc % 2][:, :],
                                                        ps_k[tc % 3][:, :], AF.Copy))
                            wait(te, "S", sc)
                            tr = inc("T", te.matmul(ps_r[tc % 2][:, :], p2c[:, :],
                                                    raw[tc % 2][:, :],
                                                    start=True, stop=True))
                            wait(ve, "S", sc)
                            wait(ve, "C", rope_ready)
                            ve.tensor_tensor(tmpa[:, :], raw[tc % 2][:, :],
                                             cosT[:, tc * 512:(tc + 1) * 512], ALU.mult)
                            wait(ve, "T", tr)
                            ve.tensor_copy(rot[tc % 2][:, :], ps_r[tc % 2][:, :])
                            ve.tensor_tensor(tmpb[:, :], rot[tc % 2][:, :],
                                             sinT[:, tc * 512:(tc + 1) * 512], ALU.mult)
                            vv = inc("V", ve.tensor_tensor(
                                dstT[:, tc * 512:(tc + 1) * 512],
                                tmpa[:, :], tmpb[:, :], ALU.add))
                            rope_done[tc % 3] = vv
                            raw_rd[tc % 2] = vv
                        if is_q:
                            wait(sy, "V", vv)
                            q_spill[dt_i % 2] = inc(
                                "X" if dt_i % 2 == 0 else "X1",
                                sy.dma_start(out=q_scr[dt_i], in_=dstT[:, :]))
                    if is_q:
                        wait(sy, "X")
                        wait(sy, "X1")

                qk_phase(False)
                qk_phase(True)
                if debug:
                    wait(sy, "V")
                    inc("X", sy.dma_start(out=dbg["kT0"][:, :], in_=kT[0][:, :]))
                    for i_ in range(NF):
                        inc("X", sy.dma_start(out=dbg["k_all"][i_], in_=kT[i_][:, :]))
            nc.all_engine_barrier()

        with ExitStack() as sc3:
            SB3 = lambda n, shp, dt: sc3.enter_context(nc.sbuf_tensor(n, shp, dt))
            PS3 = lambda n, shp, dt: sc3.enter_context(nc.psum_tensor(n, shp, dt))
            v_aug = [SB3(f"vaug{tt}", [128, 32 * 65], BF16) for tt in range(NT)]
            qbuf = [SB3(f"qbuf{qd}", [128, 512], BF16) for qd in range(NF)]
            probs = [[SB3(f"probs{pp}_{kt}", [128, 512], BF16) for kt in range(NT)]
                     for pp in range(2)]
            attn_tok = [SB3(f"attn_tok{qs}", [128, D], BF16) for qs in range(4)]
            atb = [SB3(f"atb{i}", [128, 128], BF16) for i in range(2)]
            rec = SB3("rec", [128, 16], F32)
            ps_s = [PS3(f"ps_s{i}", [128, 512], F32) for i in range(3)]
            ps_av = [PS3(f"ps_av{i}", [128, 65], F32) for i in range(3)]
            ps_tr3 = [PS3(f"ps_tr3{i}", [128, 128], BF16) for i in range(2)]

            for tt in range(NT):
                inc("C", sy.dma_start(out=v_aug[tt][:, :], in_=v_scr[tt]))
            v_ready = cnt["C"]

            exp_done = np.zeros((2, NT), dtype=np.int64)
            exp_bank = [0, 0, 0]
            mask_done = np.zeros((2, NT), dtype=np.int64)
            av_last_user = np.zeros((2, NT), dtype=np.int64)
            norm_done = [0, 0, 0]
            atb_dma = [0, 0]
            first = True
            for qc in range(NQC):
                if qc >= 1:
                    wait(sy, "T")
                qld = [0] * NF
                for qd in range(NF):
                    qld[qd] = inc("C", sy.dma_start(
                        out=qbuf[qd][:, :], in_=q_scr[qd][:, qc * 512:(qc + 1) * 512]))
                qld = [qld[NF - 1]] * NF
                nkt = 4 * qc + 4
                hseq = (list(range(H)) if qc == 0 else []) + list(range(H))
                for h_i in hseq:
                    pp = h_i % 2
                    kd, off = h_i // 2, (h_i % 2) * 64
                    for kt in range(nkt):
                        jq0 = max(0, (kt - 4 * qc)) * 128
                        wait(te, "C", qld[kd])
                        if first:
                            wait(te, "C", v_ready)
                            first = False
                        wait(te, "S", exp_bank[kt % 3])
                        tg = inc("T", te.matmul(
                            ps_s[kt % 3][:, jq0:512],
                            kT[kd][off:off + 64, kt * 128:(kt + 1) * 128],
                            qbuf[kd][off:off + 64, jq0:512], start=True, stop=True))
                        wait(se, "T", tg)
                        wait(se, "T", av_last_user[pp][kt])
                        ex = inc("S", se.activation(probs[pp][kt][:, jq0:512],
                                                    ps_s[kt % 3][:, jq0:512],
                                                    AF.Exp, scale=0.125))
                        exp_done[pp][kt] = ex
                        exp_bank[kt % 3] = ex
                        if kt >= 4 * qc:
                            j = kt - 4 * qc
                            wait(ve, "S", ex)
                            mask_done[pp][kt] = inc("V", ve.tensor_tensor(
                                probs[pp][kt][:, j * 128:(j + 1) * 128],
                                probs[pp][kt][:, j * 128:(j + 1) * 128],
                                tric[:, :], ALU.mult))
                    rr_ = 0
                    for qs in range(4):
                        qs128 = qc * 4 + qs
                        wait(te, "S", exp_done[pp][qs128])
                        wait(te, "V", mask_done[pp][qs128])
                        wait(te, "V", norm_done[qs % 3])
                        grp = None
                        for kt in range(qs128 + 1):
                            grp = te.matmul(ps_av[qs % 3][:, :],
                                            probs[pp][kt][:, qs * 128:(qs + 1) * 128],
                                            v_aug[kt][:, h_i * 65:h_i * 65 + 65],
                                            start=(kt == 0), stop=(kt == qs128))
                        avg = inc("T", grp)
                        for kt in range(qs128 + 1):
                            av_last_user[pp][kt] = avg
                        wait(ve, "T", avg)
                        rr_ = inc("R", ve.reciprocal(rec[:, qs * 4:qs * 4 + 1],
                                                     ps_av[qs % 3][:, 64:65]))
                        if qs >= 1:
                            norm_done[(qs - 1) % 3] = inc("V", ve.tensor_scalar(
                                attn_tok[qs - 1][:, h_i * 64:(h_i + 1) * 64],
                                ps_av[(qs - 1) % 3][:, 0:64],
                                rec[:, (qs - 1) * 4:(qs - 1) * 4 + 1], None, ALU.mult))
                    ve.wait_ge(sR, rr_)
                    norm_done[3 % 3] = inc("V", ve.tensor_scalar(
                        attn_tok[3][:, h_i * 64:(h_i + 1) * 64],
                        ps_av[3 % 3][:, 0:64],
                        rec[:, 12:13], None, ALU.mult))
                if debug and qc == 0:
                    wait(sy, "V")
                    inc("X", sy.dma_start(out=dbg["atok0"][:, :], in_=attn_tok[0][:, :]))
                    wait(sy, "X")
                wait(te, "V")
                tr_evac = [0, 0]
                for qs in range(4):
                    for dt_i in range(NF):
                        i2 = dt_i % 2
                        wait(te, "V", tr_evac[i2])
                        tg = inc("T", te.transpose(
                            ps_tr3[i2][:, :],
                            attn_tok[qs][:, dt_i * 128:(dt_i + 1) * 128], ident[:, :]))
                        wait(ve, "T", tg)
                        wait(ve, "X" if i2 == 0 else "X1", atb_dma[i2])
                        tr_evac[i2] = inc("V", ve.tensor_copy(atb[i2][:, :],
                                                              ps_tr3[i2][:, :]))
                        wait(sy, "V", tr_evac[i2])
                        atb_dma[i2] = inc("X" if i2 == 0 else "X1", sy.dma_start(
                            out=at_scr[dt_i][:, qc * 512 + qs * 128:
                                             qc * 512 + (qs + 1) * 128],
                            in_=atb[i2][:, :]))
            wait(sy, "X")
            wait(sy, "X1")
            if debug:
                inc("X", sy.dma_start(out=dbg["att0"][:, :], in_=at_scr[0]))
                inc("X", sy.dma_start(out=dbg["q0"][:, :], in_=q_scr[0]))
                inc("X", sy.dma_start(out=dbg["v0"][:, :], in_=v_scr[0]))
                for i_ in range(NF):
                    inc("X", sy.dma_start(out=dbg["att_all"][i_], in_=at_scr[i_]))
                    inc("X", sy.dma_start(out=dbg["q_all"][i_], in_=q_scr[i_]))
                for i_ in range(NT):
                    inc("X", sy.dma_start(out=dbg["v_all"][i_], in_=v_scr[i_]))
        nc.all_engine_barrier()

    with ExitStack() as sc4:
        SB4 = lambda n, shp, dt: sc4.enter_context(nc.sbuf_tensor(n, shp, dt))
        PS4 = lambda n, shp, dt: sc4.enter_context(nc.psum_tensor(n, shp, dt))
        woT = SB4("woT", [128, NF * 2048], BF16)
        attb = [SB4(f"attb{dt_i}", [128, 512], BF16) for dt_i in range(NF)]
        s_stage = [SB4(f"s_stage{i}", [128, D], BF16) for i in range(2)]
        ps_wo = [PS4(f"ps_wo{i}", [128, 512], F32) for i in range(4)]
        wo_done = 0
        for g4 in range(4):
            wo_done = inc("W", ge.dma_start(
                out=woT[:, g4 * 4 * 2048:(g4 + 1) * 4 * 2048].rearrange(
                    "p (f j) -> p f j", f=4),
                in_=wo_blk[:, g4 * 4:(g4 + 1) * 4]))
        sp_done = [0, 0]
        ev_bank = [0, 0, 0, 0]
        for qc in range(NQC):
            if qc >= 1:
                wait(sy, "T")
            for dt_i in range(NF):
                inc("C", sy.dma_start(out=attb[dt_i][:, :],
                                      in_=at_scr[dt_i][:, qc * 512:(qc + 1) * 512]))
            att_ready = cnt["C"]
            for qs in range(4):
                ev = 0
                for oc in range(4):
                    wait(te, "W", wo_done)
                    wait(te, "C", att_ready)
                    wait(te, "S", ev_bank[oc])
                    grp = None
                    for fc in range(NF):
                        grp = te.matmul(
                            ps_wo[oc][:, :],
                            attb[fc][:, qs * 128:(qs + 1) * 128],
                            woT[:, fc * 2048 + oc * 512:fc * 2048 + (oc + 1) * 512],
                            start=(fc == 0), stop=(fc == NF - 1))
                    tg = inc("T", grp)
                    wait(se, "T", tg)
                    if oc == 0:
                        wait(se, "X" if qs % 2 == 0 else "X1", sp_done[qs % 2])
                    ev = inc("S", se.activation(
                        s_stage[qs % 2][:, oc * 512:(oc + 1) * 512],
                        ps_wo[oc][:, :], AF.Copy))
                    ev_bank[oc] = ev
                wait(sy, "S", ev)
                sp_done[qs % 2] = inc("X" if qs % 2 == 0 else "X1",
                                      sy.dma_start(out=s_scr[qc * 4 + qs],
                                                   in_=s_stage[qs % 2][:, :]))
        wait(sy, "X")
        wait(sy, "X1")
        if debug:
            inc("X", sy.dma_start(out=dbg["s0"][:, :], in_=s_scr[0]))
            for i_ in range(NT):
                inc("X", sy.dma_start(out=dbg["s_all"][i_], in_=s_scr[i_]))
    nc.all_engine_barrier()

    for th in range(NHALF):
        tb = th * TW
        with ExitStack() as sc5h:
            SBH = lambda n, shp, dt: sc5h.enter_context(nc.sbuf_tensor(n, shp, dt))
            hbuf = [SBH(f"h{th}_{it}", [128, TW], BF16) for it in range(NI)]

            with ExitStack() as sc6:
                SB6 = lambda n, shp, dt: sc6.enter_context(nc.sbuf_tensor(n, shp, dt))
                PS6 = lambda n, shp, dt: sc6.enter_context(nc.psum_tensor(n, shp, dt))
                ybuf = [SB6(f"y{th}_{fc}", [128, TW], BF16) for fc in range(NF)]
                xnb = [SB6(f"xnb{th}_{i}", [128, TW], BF16) for i in range(2)]
                wub = [SB6(f"wub{th}_{i}", [128, D], BF16) for i in range(2)]
                wgb = [SB6(f"wgb{th}_{i}", [128, D], BF16) for i in range(2)]
                sigb = [SB6(f"sigb{th}_{i}", [128, 512], F32) for i in range(2)]
                t1b = [SB6(f"t1b{th}_{i}", [128, 512], F32) for i in range(2)]
                ps_u = [PS6(f"ps_u{th}_{i}", [128, 512], F32) for i in range(4)]
                ps_g = [PS6(f"ps_g{th}_{i}", [128, 512], F32) for i in range(4)]

                xn_used = [0, 0]
                yev_bank = [0, 0, 0, 0]
                yev = 0
                for fc in range(NF):
                    xk = "X" if fc % 2 == 0 else "X1"
                    if fc >= 2:
                        wait(sy, "T", xn_used[fc % 2])
                    xld = inc(xk, sy.dma_start(out=xnb[fc % 2][:, :],
                                               in_=xn_scr[fc][:, tb:tb + TW]))
                    for t2 in range(2):
                        bk = (fc * 2 + t2) % 4
                        wait(te, xk, xld)
                        wait(te, "S", yev_bank[bk])
                        tg = inc("T", te.matmul(ps_u[bk][:, :],
                                                h128su[:, fc * 128:(fc + 1) * 128],
                                                xnb[fc % 2][:, t2 * 512:(t2 + 1) * 512],
                                                start=True, stop=True))
                        xn_used[fc % 2] = tg
                        wait(se, "T", tg)
                        yev = inc("S", se.activation(
                            ybuf[fc][:, t2 * 512:(t2 + 1) * 512],
                            ps_u[bk][:, :], AF.Copy, scale=C2048))
                        yev_bank[bk] = yev
                y_ready_t2 = [0, 0]
                wait(ve, "S", yev)
                for t2c in range(2):
                    c0, c1 = t2c * 512, (t2c + 1) * 512
                    hh = 1
                    while hh < NF:
                        for base in range(0, NF, 2 * hh):
                            for j in range(hh):
                                a, b = ybuf[base + j], ybuf[base + hh + j]
                                ve.tensor_tensor(a[:, c0:c1], a[:, c0:c1], b[:, c0:c1],
                                                 ALU.add)
                                ve.scalar_tensor_tensor(b[:, c0:c1], b[:, c0:c1], -2.0,
                                                        a[:, c0:c1], ALU.mult, ALU.add)
                        hh *= 2
                    y_ready_t2[t2c] = inc("V", ve.tensor_copy(tmp_pc[:, 0:1],
                                                              tmp_pc[:, 0:1]))
                if debug and th == 0:
                    wait(sy, "V", y_ready_t2[1])
                    inc("X", sy.dma_start(out=dbg["y0"][:, :], in_=ybuf[0][:, :]))

                wug_used = [0, 0]
                hdone_bank = [0, 0, 0, 0]
                sig_rd = [0, 0]
                hdone = 0
                for it in range(NI):
                    wk_ = "W" if it % 2 == 0 else "W1"
                    if it >= 2:
                        wait(ge, "T", wug_used[it % 2])
                    wu = inc(wk_, ge.dma_start(out=wub[it % 2][:, :].rearrange(
                        "p (f j) -> p f j", f=NF), in_=wu_blk[:, it]))
                    wg = inc(wk_, ge.dma_start(out=wgb[it % 2][:, :].rearrange(
                        "p (f j) -> p f j", f=NF), in_=wg_blk[:, it]))
                    for t2 in range(2):
                        bk = (it * 2 + t2) % 4
                        wait(te, wk_, wg)
                        wait(te, "V", y_ready_t2[t2])
                        wait(te, "V", hdone_bank[bk])
                        for fc in range(NF):
                            te.matmul(ps_u[bk][:, :],
                                      wub[it % 2][:, fc * 128:(fc + 1) * 128],
                                      ybuf[fc][:, t2 * 512:(t2 + 1) * 512],
                                      start=(fc == 0), stop=(fc == NF - 1))
                        gg = None
                        for fc in range(NF):
                            gg = te.matmul(ps_g[bk][:, :],
                                           wgb[it % 2][:, fc * 128:(fc + 1) * 128],
                                           ybuf[fc][:, t2 * 512:(t2 + 1) * 512],
                                           start=(fc == 0), stop=(fc == NF - 1))
                        tg = inc("T", gg)
                        wug_used[it % 2] = tg
                        wait(se, "T", tg)
                        wait(se, "V", sig_rd[t2])
                        sg_ = inc("S", se.activation(sigb[t2][:, :], ps_g[bk][:, :],
                                                     AF.Sigmoid,
                                                     scale=wsg64[:, it:it + 1]))
                        wait(ve, "S", sg_)
                        ve.scalar_tensor_tensor(t1b[t2][:, :], ps_g[bk][:, :],
                                                wsg64[:, it:it + 1], sigb[t2][:, :],
                                                ALU.mult, ALU.mult)
                        hdone = inc("V", ve.scalar_tensor_tensor(
                            hbuf[it][:, t2 * 512:(t2 + 1) * 512], ps_u[bk][:, :],
                            wsu64su[:, it:it + 1], t1b[t2][:, :], ALU.mult, ALU.mult))
                        hdone_bank[bk] = hdone
                        sig_rd[t2] = hdone

                hev_bank = [0, 0, 0, 0]
                hev_t2 = [0, 0]
                h_ready_t2 = [0, 0]
                hev = 0
                wait(te, "V", hdone)
                for t2 in range(2):
                    for it in range(NI):
                        bk = (t2 * NI + it) % 4
                        wait(te, "S", hev_bank[bk])
                        if t2 == 1 and it < 4:
                            pass
                        tg = inc("T", te.matmul(ps_u[bk][:, :], h128c[:, :],
                                                hbuf[it][:, t2 * 512:(t2 + 1) * 512],
                                                start=True, stop=True))
                        wait(se, "T", tg)
                        hev = inc("S", se.activation(
                            hbuf[it][:, t2 * 512:(t2 + 1) * 512],
                            ps_u[bk][:, :], AF.Copy, scale=C8192))
                        hev_bank[bk] = hev
                    hev_t2[t2] = hev
                    c0, c1 = t2 * 512, (t2 + 1) * 512
                    wait(ve, "S", hev_t2[t2])
                    hh = 1
                    while hh < NI:
                        for base in range(0, NI, 2 * hh):
                            for j in range(hh):
                                a, b = hbuf[base + j], hbuf[base + hh + j]
                                ve.tensor_tensor(a[:, c0:c1], a[:, c0:c1], b[:, c0:c1],
                                                 ALU.add)
                                ve.scalar_tensor_tensor(b[:, c0:c1], b[:, c0:c1], -2.0,
                                                        a[:, c0:c1], ALU.mult, ALU.add)
                        hh *= 2
                    h_ready_t2[t2] = inc("V", ve.tensor_copy(tmp_pc[:, 0:1],
                                                             tmp_pc[:, 0:1]))
                if debug and th == 0:
                    wait(sy, "V", h_ready_t2[1])
                    inc("X", sy.dma_start(out=dbg["h0"][:, :], in_=hbuf[0][:, :]))

            with ExitStack() as sc7:
                SB7 = lambda n, shp, dt: sc7.enter_context(nc.sbuf_tensor(n, shp, dt))
                PS7 = lambda n, shp, dt: sc7.enter_context(nc.psum_tensor(n, shp, dt))
                wdb = [SB7(f"wdb{th}_{i}", [128, 32 * 128], BF16) for i in range(2)]
                gstage = [SB7(f"gstage{th}_{i}", [128, TW], BF16) for i in range(2)]
                sigd = [SB7(f"sigd{th}_{i}", [128, 512], F32) for i in range(4)]
                ps_d = [PS7(f"ps_d{th}_{i}", [128, 512], F32) for i in range(8)]
                wait(ge, "T")
                gdone_bank = [0] * 8
                wd_used = [0, 0]
                sigd_rd = [0, 0, 0, 0]
                for t2 in range(2):
                    g_spill2 = [0, 0]
                    for ot in range(NO):
                        bk = (t2 * 16 + ot) % 8
                        wd_done = [0, 0]
                        for hf in range(2):
                            if t2 + ot > 0:
                                wait(ge, "T", wd_used[hf])
                            wd_done[hf] = inc("W" if hf == 0 else "W1", ge.dma_start(
                                out=wdb[hf][:, :].rearrange("p (i j) -> p i j", i=32),
                                in_=wd_blk[:, ot, hf * 32:(hf + 1) * 32]))
                        wait(te, "V", h_ready_t2[t2])
                        wait(te, "V", gdone_bank[bk])
                        grp = None
                        for it in range(NI):
                            hf, il = it // 32, it % 32
                            if il == 0:
                                wait(te, "W" if hf == 0 else "W1", wd_done[hf])
                            grp = te.matmul(ps_d[bk][:, :],
                                            wdb[hf][:, il * 128:(il + 1) * 128],
                                            hbuf[it][:, t2 * 512:(t2 + 1) * 512],
                                            start=(it == 0), stop=(it == NI - 1))
                        tg = inc("T", grp)
                        wd_used[0] = wd_used[1] = tg
                        wait(se, "T", tg)
                        wait(se, "V", sigd_rd[ot % 4])
                        sd = inc("S", se.activation(sigd[ot % 4][:, :], ps_d[bk][:, :],
                                                    AF.Sigmoid,
                                                    scale=wsdp64[:, ot:ot + 1]))
                        wait(ve, "S", sd)
                        wait(ve, "X" if ot % 2 == 0 else "X1", g_spill2[ot % 2])
                        gdone = inc("V", ve.scalar_tensor_tensor(
                            gstage[ot % 2][:, t2 * 512:(t2 + 1) * 512], ps_d[bk][:, :],
                            wsdp64[:, ot:ot + 1], sigd[ot % 4][:, :],
                            ALU.mult, ALU.mult))
                        gdone_bank[bk] = gdone
                        sigd_rd[ot % 4] = gdone
                        wait(sy, "V", gdone)
                        g_spill2[ot % 2] = inc("X" if ot % 2 == 0 else "X1",
                                               sy.dma_start(
                            out=g_scr[ot][:, t2 * 512:(t2 + 1) * 512],
                            in_=gstage[ot % 2][:, t2 * 512:(t2 + 1) * 512]))
                if debug and th == 0:
                    wait(sy, "X")
                    inc("X", sy.dma_start(out=dbg["g0"][:, :], in_=g_scr[0]))
                wait(sy, "X")
                wait(sy, "X1")

        with ExitStack() as sc8:
            SB8 = lambda n, shp, dt: sc8.enter_context(nc.sbuf_tensor(n, shp, dt))
            PS8 = lambda n, shp, dt: sc8.enter_context(nc.psum_tensor(n, shp, dt))
            gcol = [SB8(f"gcol{th}_{i}", [128, NO * 128], BF16) for i in range(2)]
            gtok = [SB8(f"gtok{th}_{i}", [128, D], BF16) for i in range(2)]
            xbuf2 = [SB8(f"xbuf2{th}_{i}", [128, D], F32) for i in range(2)]
            sbuf2 = [SB8(f"sbuf2{th}_{i}", [128, D], BF16) for i in range(2)]
            sgb = SB8(f"sgb{th}", [128, D], F32)
            obuf = [SB8(f"obuf{th}_{i}", [128, D], F32) for i in range(2)]
            ps_t8 = [PS8(f"ps_t8{th}_{i}", [128, 128], BF16) for i in range(4)]
            ost = [0, 0]
            comb = [0, 0]
            gused = [0, 0]
            gev_bank = [0, 0, 0, 0]
            for ts in range(8):
                tt = th * 8 + ts
                xk = "X" if ts % 2 == 0 else "X1"
                wait(sy, "V", comb[ts % 2])
                wait(sy, "T", gused[ts % 2])
                gl = 0
                for ot in range(NO):
                    gl = inc(xk, sy.dma_start(
                        out=gcol[ts % 2][:, ot * 128:(ot + 1) * 128],
                        in_=g_scr[ot][:, ts * 128:(ts + 1) * 128]))
                xl = inc(xk, sy.dma_start(out=xbuf2[ts % 2][:, :],
                                          in_=xb[tt * 128:(tt + 1) * 128, :]))
                sl = inc(xk, sy.dma_start(out=sbuf2[ts % 2][:, :], in_=s_scr[tt]))
                gev = 0
                for ot in range(NO):
                    wait(te, xk, sl)
                    wait(te, "V", gev_bank[ot % 4])
                    tg = inc("T", te.transpose(ps_t8[ot % 4][:, :],
                                               gcol[ts % 2][:, ot * 128:(ot + 1) * 128],
                                               ident[:, :]))
                    gused[ts % 2] = tg
                    wait(ve, "T", tg)
                    if ot == 0:
                        wait(ve, xk, ost[ts % 2])
                    gev = inc("V", ve.tensor_copy(
                        gtok[ts % 2][:, ot * 128:(ot + 1) * 128], ps_t8[ot % 4][:, :]))
                    gev_bank[ot % 4] = gev
                wait(ve, xk, sl)
                ve.tensor_tensor(sgb[:, :], sbuf2[ts % 2][:, :], gtok[ts % 2][:, :],
                                 ALU.mult)
                comb[ts % 2] = inc("V", ve.tensor_tensor(
                    obuf[ts % 2][:, :], xbuf2[ts % 2][:, :], sgb[:, :], ALU.add))
                wait(sy, "V", comb[ts % 2])
                ost[ts % 2] = inc(xk, sy.dma_start(
                    out=out[tt * 128:(tt + 1) * 128, :], in_=obuf[ts % 2][:, :]))
            wait(sy, "X")
            wait(sy, "X1")
        nc.all_engine_barrier()

    ge.sem_clear(rr)
    nc.all_engine_barrier()
    ctx.close()
    return nc


def host_prep(inputs):
    wqkv = np.asarray(inputs["L0_attn_wqkv"], np.float32)
    wo = np.asarray(inputs["L0_attn_wo"], np.float32)
    wu = np.asarray(inputs["L0_ffn_W_up"], np.float32)
    wg = np.asarray(inputs["L0_ffn_W_gate"], np.float32)
    wd = np.asarray(inputs["L0_ffn_W_down"], np.float32)
    su_ug = np.asarray(inputs["L0_ffn_SU_ug"], np.float32)
    su_dp = np.asarray(inputs["L0_ffn_SU_dp"], np.float32)
    wsc_ug = np.asarray(inputs["L0_ffn_Wscale_ug"], np.float32)
    wsc_dp = np.asarray(inputs["L0_ffn_Wscale_dp"], np.float32)

    qk = wqkv[0:4096]
    wqkv_blk = np.ascontiguousarray(qk.reshape(32, 128, 16, 128).transpose(3, 0, 2, 1))
    wv = wqkv[4096:6144]
    wv_blk = np.ascontiguousarray(wv.reshape(4, 512, 16, 128).transpose(3, 0, 2, 1))
    wo_blk = np.ascontiguousarray(wo.T.reshape(16, 128, 2048).transpose(1, 0, 2))
    wu_blk = np.ascontiguousarray(wu.reshape(64, 128, 16, 128).transpose(3, 0, 2, 1))
    wg_blk = np.ascontiguousarray(wg.reshape(64, 128, 16, 128).transpose(3, 0, 2, 1))
    wd_blk = np.ascontiguousarray(wd.reshape(16, 128, 64, 128).transpose(3, 0, 2, 1))
    consts = build_host_consts()
    base = dict(
        wqkv_blk=wqkv_blk, wv_blk=wv_blk, wo_blk=wo_blk, wu_blk=wu_blk,
        wg_blk=wg_blk, wd_blk=wd_blk,
        su_ug_pt=np.ascontiguousarray(su_ug.reshape(16, 128).T),
        su_dp_pt=np.ascontiguousarray(su_dp.reshape(64, 128).T),
        wsu_pt=np.ascontiguousarray(wsc_ug[:INTER].reshape(64, 128).T),
        wsg_pt=np.ascontiguousarray(wsc_ug[INTER:].reshape(64, 128).T),
        wsdp_pt=np.ascontiguousarray(wsc_dp.reshape(16, 128).T),
        ident_in=consts["ident"], h128_in=consts["h128"], p2_in=consts["p2"],
        tri_in=consts["tri"], cos_in=consts["cosT"], sin_in=consts["sinT"])
    x = np.asarray(inputs["L0_attn_x"], np.float32)
    maps = []
    for b in range(8):
        m = dict(base)
        m["xb"] = np.ascontiguousarray(x[b])
        maps.append(m)
    return maps


def kernel(**inputs):
    maps = host_prep(inputs)
    nc = build_program(debug=False)
    res = run_bass_kernel_spmd(nc, maps, core_ids=list(range(8)))
    return np.stack([np.asarray(r["out"], np.float32) for r in res.results], axis=0)


